# version 01
# baseline (speedup 1.0000x reference)
import numpy as np
from contextlib import ExitStack
import ml_dtypes
import concourse.bass as bass
import concourse.mybir as mybir
from concourse.bass_utils import run_bass_kernel_spmd

F32 = mybir.dt.float32
BF16 = mybir.dt.bfloat16
ALU = mybir.AluOpType
ACT = mybir.ActivationFunctionType
AX = mybir.AxisListType

D = 1024
NT = 2048
DHID = 4096
TB = 512
NTB = NT // TB
NCH = D // 128
NOC = DHID // 128
DBG_P2BARRIER = False
DBG_CHAIN_DEFER = False

CONST_VALS = [
    0.5, -0.5, 2.0, 1.0 / D, 1e-5, -1e30, 1e-12, 32.0, 1.0 / 32.0,
    float(128) ** 0.5, 0.125, -1.0, -0.125, 0.0, 1.0, 1024.0,
    float(D) ** -0.5,
]


class Sch:
    ENGS = ("sync", "tensor", "vector", "scalar", "gpsimd")

    def __init__(self, nc):
        self.nc = nc
        self.q = {e: [] for e in self.ENGS}
        self.sems = {}
        self.cnt = {}

    def sem(self, name):
        if name not in self.sems:
            self.sems[name] = self.nc.alloc_semaphore(name)
            self.cnt[name] = 0
        return self.sems[name]

    def wait(self, eng, name, val):
        if val <= 0:
            return
        h = self.sem(name)
        self.q[eng].append(lambda E, h=h, v=val: E.wait_ge(h, v))

    def op(self, eng, fn, inc=None, n=1):
        if inc is None:
            self.q[eng].append(fn)
            return None
        h = self.sem(inc)
        self.cnt[inc] += n

        def g(E, fn=fn, h=h, n=n):
            fn(E).then_inc(h, n)
        self.q[eng].append(g)
        return self.cnt[inc]

    def dma(self, eng, out, in_, inc):
        return self.op(eng, lambda E, o=out, i=in_: E.dma_start(o, i), inc, 16)

    def fence(self, eng):
        for s in ("v", "t", "s", "g"):
            if self.cnt.get(s, 0):
                self.wait(eng, s, self.cnt[s])

    def selfwait(self, eng, name):
        self.wait(eng, name, self.cnt[name])

    def barrier(self):
        snap = {k: v for k, v in self.cnt.items()}
        for e in self.ENGS:
            for name, v in snap.items():
                self.wait(e, name, v)

    def finalize(self, block):
        for e in self.ENGS:
            def body(E, e=e):
                for f in self.q[e]:
                    f(E)
            getattr(block, e)(body)


def sb(stack, nc, name, shape, dtype):
    g = stack.enter_context(nc.sbuf_tensor(name, shape, dtype))
    return g.ap()


class Prelude:

    def __init__(self, nc, sch):
        self.nc = nc
        self.consts = nc.alloc_sbuf_tensor("c_consts", [128, len(CONST_VALS)], F32).ap()
        self.ones_col = nc.alloc_sbuf_tensor("c_ones_col", [128, 2], F32).ap()
        self.ones_row = nc.alloc_sbuf_tensor("c_ones_row", [1, 128], F32).ap()
        self.ident = nc.alloc_sbuf_tensor("c_ident", [128, 128], F32).ap()
        self.s1 = nc.alloc_sbuf_tensor("c_s1", [128, NOC], F32).ap()
        self.s2 = nc.alloc_sbuf_tensor("c_s2", [128, NOC], F32).ap()
        self.sh1 = nc.alloc_sbuf_tensor("c_sh1", [128, NOC], F32).ap()
        self.sh2 = nc.alloc_sbuf_tensor("c_sh2", [128, NOC], F32).ap()
        self.s3h = nc.alloc_sbuf_tensor("c_s3h", [128, NCH], F32).ap()
        self.ci = {v: i for i, v in enumerate(CONST_VALS)}

    def cap(self, val):
        return self.consts[0:1, self.ci[val]:self.ci[val] + 1]

    def cpp(self, val):
        return self.consts[:, self.ci[val]:self.ci[val] + 1]

    def emit_load(self, sch, dram):
        nc = self.nc
        sch.dma("sync", self.consts, dram["consts"], "ds")
        sch.dma("sync", self.ident, dram["ident"], "ds")
        sch.dma("sync", self.s1, dram["ffn_s1"], "ds")
        sch.dma("sync", self.s2, dram["ffn_s2"], "ds")
        sch.dma("sync", self.sh1, dram["ffn_sh1"], "ds")
        sch.dma("sync", self.sh2, dram["ffn_sh2"], "ds")
        sch.dma("sync", self.s3h, dram["ffn_s3"], "ds")
        ds0 = sch.cnt["ds"]
        sch.wait("vector", "ds", ds0)
        sch.op("vector", lambda E: E.memset(self.ones_col, 1.0), "v")
        sch.op("vector", lambda E: E.memset(self.ones_row, 1.0), "v")
        v_half = sch.op("vector", lambda E: E.tensor_scalar_mul(
            self.s3h, self.s3h, 0.5), "v")
        for v in CONST_VALS:
            self.nc.const_aps.register(F32, v, self.cpp(v))
        for e in Sch.ENGS:
            sch.wait(e, "ds", ds0)
            sch.wait(e, "v", v_half)


def emit_ffn0(sch, P, stack, dram_w, x_src, x_dst, tag):
    nc = sch.nc
    x_blk = sb(stack, nc, tag + "_x", [128, NCH * TB], F32)
    xn_hi = sb(stack, nc, tag + "_xnh", [128, NCH * TB], BF16)
    xn_lo = sb(stack, nc, tag + "_xnl", [128, NCH * TB], BF16)
    h_hi = sb(stack, nc, tag + "_hh", [128, NOC * TB], BF16)
    h_lo = sb(stack, nc, tag + "_hl", [128, NOC * TB], BF16)
    g_hi = sb(stack, nc, tag + "_gh", [128, NOC * TB], BF16)
    g_lo = sb(stack, nc, tag + "_gl", [128, NOC * TB], BF16)
    w1buf = [sb(stack, nc, tag + f"_w1{i}", [128, 1024], BF16) for i in range(2)]
    w2buf = [sb(stack, nc, tag + f"_w2{i}", [128, 1024], BF16) for i in range(2)]
    hbuf = [sb(stack, nc, tag + f"_hb{i}", [128, 4096], BF16) for i in range(2)]
    w3buf = hbuf
    tmp = [sb(stack, nc, tag + f"_t{i}", [128, TB], F32) for i in range(2)]
    tsil = [sb(stack, nc, tag + f"_ts{i}", [128, TB], F32) for i in range(2)]
    xnew = tsil
    stat = sb(stack, nc, tag + "_st", [1, 4 * TB], F32)
    pb = P.pb

    v_E_last = 0
    t_prev_block_stats = 0
    da_hist = []
    t_B_byb = {}
    t_C_byb = {}
    t_E_byb = {}
    v_xn_prev = 0

    for tb in range(NTB):
        col0 = tb * TB
        if v_E_last:
            sch.wait("sync", "v", v_E_last)
        for c in range(NCH):
            sch.dma("sync", x_blk[:, c * TB:(c + 1) * TB],
                    x_src[c * 128:(c + 1) * 128, col0:col0 + TB], "ds")
        ds_x = sch.cnt["ds"]

        sch.wait("scalar", "ds", ds_x)
        if v_xn_prev:
            sch.wait("tensor", "v", v_xn_prev)
        t_sq = {}
        for c in range(NCH):
            if c >= 2:
                sch.wait("scalar", "t", t_sq[c - 2])
            s_sq = sch.op("scalar", lambda E, c=c: E.activation(
                tmp[c % 2], x_blk[:, c * TB:(c + 1) * TB], ACT.Square), "s")
            sch.wait("tensor", "s", s_sq)
            if c == 0:
                sch.wait("tensor", "ds", ds_x)
            sch.op("tensor", lambda E, c=c: E.matmul(
                pb[4][0:1, :], P.ones_col[:, 0:1],
                x_blk[:, c * TB:(c + 1) * TB],
                start=(c == 0), stop=(c == NCH - 1)),
                "t" if c == NCH - 1 else None)
            t_sq[c] = sch.op("tensor", lambda E, c=c: E.matmul(
                pb[5][0:1, :], P.ones_col[:, 0:1], tmp[c % 2],
                start=(c == 0), stop=(c == NCH - 1)), "t")
        t_stats = sch.cnt["t"]

        sum_x = stat[0:1, 0:TB]
        sum_sq = stat[0:1, TB:2 * TB]
        m_sb = stat[0:1, 2 * TB:3 * TB]
        rstd_sb = stat[0:1, 3 * TB:4 * TB]
        sch.wait("vector", "t", t_stats)
        sch.op("vector", lambda E: E.tensor_copy(sum_x, pb[4][0:1, :]))
        sch.op("vector", lambda E: E.tensor_copy(sum_sq, pb[5][0:1, :]))
        sch.op("vector", lambda E: E.tensor_scalar_mul(m_sb, sum_x, 1.0 / D))
        sch.op("vector", lambda E: E.tensor_scalar_mul(sum_sq, sum_sq, 1.0 / D))
        sch.op("vector", lambda E: E.tensor_tensor(sum_x, m_sb, m_sb, ALU.mult))
        sch.op("vector", lambda E: E.tensor_tensor(sum_sq, sum_sq, sum_x, ALU.subtract))
        v_var = sch.op("vector", lambda E: E.tensor_scalar_add(
            sum_sq, sum_sq, 1e-5), "v")
        sch.wait("scalar", "v", v_var)
        s_sd = sch.op("scalar", lambda E: E.activation(sum_x, sum_sq, ACT.Sqrt), "s")
        sch.wait("vector", "s", s_sd)
        v_rstd = sch.op("vector", lambda E: E.reciprocal(rstd_sb, sum_x), "v")

        sch.wait("tensor", "v", v_rstd)
        sch.op("tensor", lambda E: E.matmul(
            pb[4], P.ones_row, m_sb, start=True, stop=True))
        t_bcast = sch.op("tensor", lambda E: E.matmul(
            pb[5], P.ones_row, rstd_sb, start=True, stop=True), "t")

        sch.wait("vector", "t", t_bcast)
        for c in range(NCH):
            xs = x_blk[:, c * TB:(c + 1) * TB]
            hi = xn_hi[:, c * TB:(c + 1) * TB]
            lo = xn_lo[:, c * TB:(c + 1) * TB]
            t0 = tmp[c % 2]
            sch.op("vector", lambda E, xs=xs, t0=t0: E.tensor_tensor(
                t0, xs, pb[4], ALU.subtract))
            sch.op("vector", lambda E, t0=t0: E.tensor_tensor(
                t0, t0, pb[5], ALU.mult))
            sch.op("vector", lambda E, hi=hi, t0=t0: E.tensor_copy(hi, t0))
            sch.op("vector", lambda E, hi=hi, lo=lo, t0=t0: E.tensor_tensor(
                lo, t0, hi, ALU.subtract), "v")
        v_xn = sch.cnt["v"]
        v_xn_prev = v_xn

        v_evac = {}
        first_B = True
        if da_hist:
            sch.wait("scalar", "da", da_hist[-1])
        for oc in range(NOC):
            if tag + "B" + str(oc % 2) in t_B_byb:
                sch.wait("gpsimd", "t", t_B_byb[tag + "B" + str(oc % 2)])
            sch.dma("gpsimd", w1buf[oc % 2], dram_w["W1P"][oc, :, :], "dg")
            dg_w = sch.dma("gpsimd", w2buf[oc % 2], dram_w["W2P"][oc, :, :],
                           "dg")
            sch.wait("tensor", "dg", dg_w)
            if first_B:
                sch.wait("tensor", "v", v_xn)
                first_B = False
            if oc >= 2:
                sch.wait("tensor", "v", v_evac[oc - 2])
            bA = pb[(oc % 2) * 2 + 0]
            bB = pb[(oc % 2) * 2 + 1]
            for bank, wb in ((bA, w1buf[oc % 2]), (bB, w2buf[oc % 2])):
                n_mm = 0
                for xp in (xn_hi, xn_lo):
                    for ic in range(NCH):
                        n_mm += 1
                        sch.op("tensor", lambda E, bank=bank, wb=wb,
                               ic=ic, xp=xp, st=(n_mm == 1),
                               sp=(n_mm == 16): E.matmul(
                            bank, wb[:, ic * 128:(ic + 1) * 128],
                            xp[:, ic * TB:(ic + 1) * TB],
                            start=st, stop=sp),
                            "t" if n_mm == 16 else None)
            t_grp = sch.cnt["t"]
            t_B_byb[tag + "B" + str(oc % 2)] = t_grp
            sch.wait("scalar", "t", t_grp)
            s_sil = sch.op("scalar", lambda E, oc=oc, bA=bA: E.activation(
                tsil[oc % 2], bA, ACT.Silu,
                scale=P.s1[:, oc:oc + 1]), "s")
            sch.wait("vector", "s", s_sil)
            hi = h_hi[:, oc * TB:(oc + 1) * TB]
            lo = h_lo[:, oc * TB:(oc + 1) * TB]
            sch.op("vector", lambda E, oc=oc, bB=bB: E.scalar_tensor_tensor(
                tmp[oc % 2], bB, P.s2[:, oc:oc + 1], tsil[oc % 2],
                ALU.mult, ALU.mult))
            sch.op("vector", lambda E, hi=hi, oc=oc: E.tensor_copy(
                hi, tmp[oc % 2]))
            v_evac[oc] = sch.op("vector", lambda E, hi=hi, lo=lo, oc=oc:
                                E.tensor_tensor(lo, tmp[oc % 2], hi,
                                                ALU.subtract), "v")
        v_hdone = sch.cnt["v"]

        def stage_CD(src_hi, src_lo, dst_hi, dst_lo, WP, scale_tile, key):
            v_done_in = sch.cnt["v"]
            v_ev = {}
            first = True
            for oc in range(NOC):
                if key + str(oc % 2) in t_C_byb:
                    sch.wait("gpsimd", "t", t_C_byb[key + str(oc % 2)])
                dg = sch.dma("gpsimd", hbuf[oc % 2], WP[oc, :, :], "dg")
                sch.wait("tensor", "dg", dg)
                if first:
                    sch.wait("tensor", "v", v_done_in)
                    first = False
                if oc >= 2:
                    sch.wait("tensor", "v", v_ev[oc - 2])
                bank = pb[oc % 2]
                n_mm = 0
                for xp in (src_hi, src_lo):
                    for ic in range(NOC):
                        n_mm += 1
                        sch.op("tensor", lambda E, bank=bank, oc=oc, ic=ic,
                               xp=xp, st=(n_mm == 1), sp=(n_mm == 64):
                               E.matmul(
                                   bank,
                                   hbuf[oc % 2][:, ic * 128:(ic + 1) * 128],
                                   xp[:, ic * TB:(ic + 1) * TB],
                                   start=st, stop=sp),
                               "t" if n_mm == 64 else None)
                t_grp = sch.cnt["t"]
                t_C_byb[key + str(oc % 2)] = t_grp
                sch.wait("scalar", "t", t_grp)
                s_g = sch.op("scalar", lambda E, oc=oc, bank=bank: E.activation(
                    tmp[oc % 2], bank, ACT.Gelu,
                    scale=scale_tile[:, oc:oc + 1]), "s")
                sch.wait("vector", "s", s_g)
                hi = dst_hi[:, oc * TB:(oc + 1) * TB]
                lo = dst_lo[:, oc * TB:(oc + 1) * TB]
                sch.op("vector", lambda E, hi=hi, oc=oc: E.tensor_copy(
                    hi, tmp[oc % 2]))
                v_ev[oc] = sch.op("vector", lambda E, hi=hi, lo=lo, oc=oc:
                                  E.tensor_tensor(lo, tmp[oc % 2], hi,
                                                  ALU.subtract), "v")

        stage_CD(h_hi, h_lo, g_hi, g_lo, dram_w["H1P"], P.sh1, tag + "CD")
        stage_CD(g_hi, g_lo, h_hi, h_lo, dram_w["H2P"], P.sh2, tag + "CD")
        v_gdone = sch.cnt["v"]

        first_E = True
        s_E = {}
        for c in range(NCH):
            if tag + "CD" + str(c % 2) in t_C_byb:
                sch.wait("gpsimd", "t", t_C_byb[tag + "CD" + str(c % 2)])
            dg = sch.dma("gpsimd", w3buf[c % 2], dram_w["W3P"][c, :, :], "dg")
            sch.wait("tensor", "dg", dg)
            if first_E:
                sch.wait("tensor", "v", v_gdone)
                first_E = False
            if c >= 2:
                sch.wait("tensor", "s", s_E[c - 2])
            bank = pb[c % 2]
            n_mm = 0
            for xp in (h_hi, h_lo):
                for ic in range(NOC):
                    n_mm += 1
                    sch.op("tensor", lambda E, bank=bank, c=c, ic=ic, xp=xp,
                           st=(n_mm == 1), sp=(n_mm == 64): E.matmul(
                        bank, w3buf[c % 2][:, ic * 128:(ic + 1) * 128],
                        xp[:, ic * TB:(ic + 1) * TB], start=st, stop=sp),
                        "t" if n_mm == 64 else None)
            t_grp = sch.cnt["t"]
            t_C_byb[tag + "CD" + str(c % 2)] = t_grp
            sch.wait("scalar", "t", t_grp)
            s_E[c] = sch.op("scalar", lambda E, c=c, bank=bank: E.activation(
                tmp[c % 2], bank, ACT.Copy, scale=P.s3h[:, c:c + 1]), "s")
            sch.wait("vector", "s", s_E[c])
            if len(da_hist) >= 2:
                sch.wait("vector", "da", da_hist[-2])
            v_x = sch.op("vector", lambda E, c=c: E.tensor_add(
                xnew[c % 2], x_blk[:, c * TB:(c + 1) * TB], tmp[c % 2]), "v")
            sch.wait("scalar", "v", v_x)
            da = sch.dma("scalar", x_dst[c * 128:(c + 1) * 128, col0:col0 + TB],
                         xnew[c % 2], "da")
            da_hist.append(da)
        v_E_last = sch.cnt["v"]

    sch.wait("scalar", "da", sch.cnt["da"])


def emit_conv(sch, P, stack, dram_w, x_src, x_dst, tag):
    nc = sch.nc
    NTC = 16
    u = sb(stack, nc, tag + "_u", [128, NTC * 1024], F32)
    fwd = sb(stack, nc, tag + "_fwd", [128, 4096], F32)
    fny = sb(stack, nc, tag + "_fny", [128, 4], F32)
    finv = sb(stack, nc, tag + "_finv", [128, 8192], F32)
    finvny = sb(stack, nc, tag + "_finvny", [1, 1024], F32)
    skip_sb = sb(stack, nc, tag + "_skip", [1, 1024], F32)
    kslab = sb(stack, nc, tag + "_ks", [128, 2048], F32)
    KS = sb(stack, nc, tag + "_KS", [128, 4 * 1024], F32)
    KSny = sb(stack, nc, tag + "_KSn", [1, 4 * 128], F32)
    US = sb(stack, nc, tag + "_US", [128, 1024], F32)
    USny = sb(stack, nc, tag + "_USn", [1, 128], F32)
    S = sb(stack, nc, tag + "_S", [128, 4 * 1024], F32)
    Sny = sb(stack, nc, tag + "_Sn", [1, 4 * 128], F32)
    yg = sb(stack, nc, tag + "_yg", [128, NTC * 128], F32)
    xc = sb(stack, nc, tag + "_xc", [128, 2048], F32)
    scr = sb(stack, nc, tag + "_scr", [128, 1024], F32)
    scr2 = sb(stack, nc, tag + "_scr2", [128, 1024], F32)
    stats = sb(stack, nc, tag + "_stats", [128, 4 * NTC], F32)
    skipb = sb(stack, nc, tag + "_skb", [128, 128], F32)
    tmp = [sb(stack, nc, tag + f"_t{i}", [128, 128], F32) for i in range(2)]
    pb = P.pb
    bankv = {}

    def twrite(b):
        sch.wait("tensor", "v", bankv.get(b, 0))
        return pb[b]

    sch.dma("sync", fwd, dram_w["FWD"], "ds")
    sch.dma("sync", fny, dram_w["FNY"], "ds")
    sch.dma("sync", finv, dram_w["FINV"], "ds")
    sch.dma("sync", finvny, dram_w["FINVNY"], "ds")
    sch.dma("sync", skip_sb, dram_w["SKIP"], "ds")
    ds_const = sch.cnt["ds"]
    sch.wait("tensor", "ds", ds_const)

    for c in range(NCH):
        ds_c = sch.dma("sync", xc, x_src[c * 128:(c + 1) * 128, :], "ds")
        sch.wait("tensor", "ds", ds_c)
        for tc in range(NTC):
            b = 4 + (tc % 2)
            bank = twrite(b)
            sch.op("tensor", lambda E, bank=bank, tc=tc: E.transpose(
                bank[:, 0:128], xc[:, tc * 128:(tc + 1) * 128], P.ident), "t")
            t_i = sch.cnt["t"]
            sch.wait("vector", "t", t_i)
            bankv[b] = sch.op("vector", lambda E, bank=bank, c=c, tc=tc:
                              E.tensor_copy(
                                  u[:, tc * 1024 + c * 128:tc * 1024 + (c + 1) * 128],
                                  bank[:, 0:128]), "v")
        sch.wait("sync", "t", sch.cnt["t"])

    for tc in range(NTC):
        sch.op("vector", lambda E, tc=tc: E.reduce_sum(
            stats[:, tc:tc + 1], u[:, tc * 1024:(tc + 1) * 1024], AX.X), "v")
    sch.selfwait("vector", "v")
    sch.op("vector", lambda E: E.tensor_scalar_mul(
        stats[:, 16:32], stats[:, 0:16], 1.0 / D), "v")
    sch.selfwait("vector", "v")
    for tc in range(NTC):
        utc = u[:, tc * 1024:(tc + 1) * 1024]
        sch.op("vector", lambda E, utc=utc, tc=tc: E.tensor_scalar_sub(
            scr, utc, stats[:, 16 + tc:17 + tc]), "v")
        sch.op("vector", lambda E, tc=tc: E.scalar_tensor_tensor(
            scr2, scr, 1.0, scr, ALU.mult, ALU.mult,
            accum_out=stats[:, 32 + tc:33 + tc]), "v")
    sch.selfwait("vector", "v")
    sch.op("vector", lambda E: E.tensor_scalar(
        stats[:, 48:64], stats[:, 32:48], 1.0 / D, 1e-5,
        ALU.mult, ALU.add), "v")
    v_var = sch.cnt["v"]
    sch.wait("scalar", "v", v_var)
    s_sd = sch.op("scalar", lambda E: E.activation(
        stats[:, 32:48], stats[:, 48:64], ACT.Sqrt), "s")
    sch.wait("vector", "s", s_sd)
    sch.op("vector", lambda E: E.reciprocal(
        stats[:, 48:64], stats[:, 32:48]), "v")
    sch.selfwait("vector", "v")
    for tc in range(NTC):
        utc = u[:, tc * 1024:(tc + 1) * 1024]
        sch.op("vector", lambda E, utc=utc, tc=tc: E.tensor_scalar(
            utc, utc, stats[:, 16 + tc:17 + tc], stats[:, 48 + tc:49 + tc],
            ALU.subtract, ALU.mult), "v")
    v_ln = sch.cnt["v"]
    sch.wait("tensor", "v", v_ln)

    t_kfft_prev = 0
    da_prev = 0
    for g in range(NCH):
        sch.wait("sync", "t", t_kfft_prev)
        ds_k = sch.dma("sync", kslab, dram_w["KTP"][g, :, :], "ds")
        sch.wait("sync", "da", da_prev)
        ds_xg = sch.dma("sync", xc, x_src[g * 128:(g + 1) * 128, :], "ds")
        bank6 = twrite(6)
        sch.op("tensor", lambda E, g=g: E.matmul(
            bank6[:, 0:128], P.ones_row, skip_sb[0:1, g * 128:(g + 1) * 128],
            start=True, stop=True), "t")
        t_skb = sch.cnt["t"]
        sch.wait("vector", "t", t_skb)
        bankv[6] = sch.op("vector", lambda E: E.tensor_copy(
            skipb, pb[6][:, 0:128]), "v")

        def fft512(rhs_tiles, out_sp, out_ny):
            for mc in range(8):
                b = mc % 2
                bank = twrite(b)
                for kc in range(4):
                    sch.op("tensor", lambda E, bank=bank, mc=mc, kc=kc,
                           r=rhs_tiles[kc]: E.matmul(
                        bank[:, 0:128],
                        fwd[:, kc * 1024 + mc * 128:kc * 1024 + (mc + 1) * 128],
                        r, start=(kc == 0), stop=(kc == 3)),
                        "t" if kc == 3 else None)
                t_g = sch.cnt["t"]
                sch.wait("vector", "t", t_g)
                bankv[b] = sch.op("vector", lambda E, bank=bank, mc=mc,
                                  o=out_sp: E.tensor_copy(
                    o[:, mc * 128:(mc + 1) * 128], bank[:, 0:128]), "v")
            bank = twrite(7)
            for kc in range(4):
                sch.op("tensor", lambda E, bank=bank, kc=kc,
                       r=rhs_tiles[kc]: E.matmul(
                    bank[0:1, 0:128], fny[:, kc:kc + 1], r,
                    start=(kc == 0), stop=(kc == 3)),
                    "t" if kc == 3 else None)
            t_g = sch.cnt["t"]
            sch.wait("vector", "t", t_g)
            bankv[7] = sch.op("vector", lambda E, o=out_ny: E.tensor_copy(
                o, pb[7][0:1, 0:128]), "v")

        sch.wait("tensor", "ds", ds_k)
        for m in range(4):
            tiles = [kslab[:, (4 * m + kc) * 128:(4 * m + kc + 1) * 128]
                     for kc in range(4)]
            fft512(tiles, KS[:, m * 1024:(m + 1) * 1024],
                   KSny[0:1, m * 128:(m + 1) * 128])
        t_kfft_prev = sch.cnt["t"]

        for j in range(4):
            tiles = [u[:, (4 * j + kc) * 1024 + g * 128:
                       (4 * j + kc) * 1024 + (g + 1) * 128] for kc in range(4)]
            fft512(tiles, US, USny)
            v_us = sch.cnt["v"]
            sch.wait("vector", "v", v_us)
            for p in range(j, 4):
                m = p - j
                Ur = US[:, 0:512]
                Ui = US[:, 512:1024]
                Kr = KS[:, m * 1024:m * 1024 + 512]
                Ki = KS[:, m * 1024 + 512:(m + 1) * 1024]
                Sr = S[:, p * 1024:p * 1024 + 512]
                Si = S[:, p * 1024 + 512:(p + 1) * 1024]
                first = (j == 0)
                sc = scr[:, 0:512]
                if first:
                    sch.op("vector", lambda E, Ur=Ur, Kr=Kr, Sr=Sr:
                           E.tensor_tensor(Sr, Ur, Kr, ALU.mult))
                    sch.op("vector", lambda E, Ui=Ui, Ki=Ki, sc=sc:
                           E.tensor_tensor(sc, Ui, Ki, ALU.mult))
                    sch.op("vector", lambda E, Sr=Sr, sc=sc:
                           E.tensor_tensor(Sr, Sr, sc, ALU.subtract))
                    sch.op("vector", lambda E, Ur=Ur, Ki=Ki, Si=Si:
                           E.tensor_tensor(Si, Ur, Ki, ALU.mult))
                    sch.op("vector", lambda E, Ui=Ui, Kr=Kr, sc=sc:
                           E.tensor_tensor(sc, Ui, Kr, ALU.mult))
                    sch.op("vector", lambda E, Si=Si, sc=sc:
                           E.tensor_tensor(Si, Si, sc, ALU.add), "v")
                else:
                    for (A, Bv, dst, op) in ((Ur, Kr, Sr, ALU.add),
                                             (Ui, Ki, Sr, ALU.subtract),
                                             (Ur, Ki, Si, ALU.add),
                                             (Ui, Kr, Si, ALU.add)):
                        sch.op("vector", lambda E, A=A, Bv=Bv, sc=sc:
                               E.tensor_tensor(sc, A, Bv, ALU.mult))
                        sch.op("vector", lambda E, dst=dst, sc=sc, op=op:
                               E.tensor_tensor(dst, dst, sc, op), "v")
                Un = USny
                Kn = KSny[0:1, m * 128:(m + 1) * 128]
                Sn = Sny[0:1, p * 128:(p + 1) * 128]
                sn = scr[0:1, 512:640]
                if first:
                    sch.op("vector", lambda E, Sn=Sn, Un=Un, Kn=Kn:
                           E.tensor_tensor(Sn, Un, Kn, ALU.mult), "v")
                else:
                    sch.op("vector", lambda E, sn=sn, Un=Un, Kn=Kn:
                           E.tensor_tensor(sn, Un, Kn, ALU.mult))
                    sch.op("vector", lambda E, Sn=Sn, sn=sn:
                           E.tensor_tensor(Sn, Sn, sn, ALU.add), "v")
            v_S = sch.cnt["v"]

            sch.wait("tensor", "v", v_S)
            for tc in range(4):
                b = 2 + (tc % 2)
                bank = twrite(b)
                terms = [(j, tc)] + ([(j - 1, tc + 4)] if j > 0 else [])
                n_tot = len(terms) * 9
                n_acc = 0
                for (p_, mcol) in terms:
                    for kc in range(8):
                        n_acc += 1
                        sch.op("tensor", lambda E, bank=bank, kc=kc, p_=p_,
                               mcol=mcol, st=(n_acc == 1): E.matmul(
                            bank[:, 0:128],
                            finv[:, kc * 1024 + mcol * 128:kc * 1024 + (mcol + 1) * 128],
                            S[:, p_ * 1024 + kc * 128:p_ * 1024 + (kc + 1) * 128],
                            start=st, stop=False))
                    n_acc += 1
                    sch.op("tensor", lambda E, bank=bank, p_=p_, mcol=mcol,
                           sp=(n_acc == n_tot): E.matmul(
                        bank[:, 0:128],
                        finvny[0:1, mcol * 128:(mcol + 1) * 128],
                        Sny[0:1, p_ * 128:(p_ + 1) * 128],
                        start=False, stop=sp),
                        "t" if n_acc == n_tot else None)
                t_y = sch.cnt["t"]
                sch.wait("vector", "t", t_y)
                ut = u[:, (4 * j + tc) * 1024 + g * 128:
                       (4 * j + tc) * 1024 + (g + 1) * 128]
                ys = yg[:, (4 * j + tc) * 128:(4 * j + tc + 1) * 128]
                sch.op("vector", lambda E, ut=ut: E.tensor_tensor(
                    tmp[0], ut, skipb, ALU.mult))
                bankv[b] = sch.op("vector", lambda E, ys=ys, bank=bank:
                                  E.tensor_add(ys, bank[:, 0:128], tmp[0]),
                                  "v")
        v_yg = sch.cnt["v"]

        sch.wait("tensor", "v", v_yg)
        sch.wait("tensor", "ds", ds_xg)
        for tc in range(NTC):
            b = 4 + (tc % 2)
            bank = twrite(b)
            sch.op("tensor", lambda E, bank=bank, tc=tc: E.transpose(
                bank[:, 0:128], yg[:, tc * 128:(tc + 1) * 128], P.ident), "t")
            t_tp = sch.cnt["t"]
            sch.wait("vector", "t", t_tp)
            bankv[b] = sch.op("vector", lambda E, bank=bank, tc=tc:
                              E.tensor_add(xc[:, tc * 128:(tc + 1) * 128],
                                           xc[:, tc * 128:(tc + 1) * 128],
                                           bank[:, 0:128]), "v")
        v_xo = sch.cnt["v"]
        sch.wait("scalar", "v", v_xo)
        da_prev = sch.dma("scalar", x_dst[g * 128:(g + 1) * 128, :], xc, "da")
    sch.wait("scalar", "da", sch.cnt["da"])
    sch.dbg = dict(u=u, S=S, KS=KS, US=US, Sny=Sny, yg=yg, stats=stats)


def prep_conv_inputs(k_full, skip):
    k2 = k_full[:, :2048]
    KTP = np.ascontiguousarray(
        k2.reshape(8, 128, 16, 128).transpose(0, 3, 2, 1))
    KTP = KTP.reshape(8, 128, 2048)
    n = 1024
    t = np.arange(512)
    f = np.arange(512)
    ang = 2 * np.pi * np.outer(t, f) / n
    FWD = np.zeros((128, 4096), dtype=np.float64)
    for kc in range(4):
        tt = t[kc * 128:(kc + 1) * 128]
        a = 2 * np.pi * np.outer(tt, f) / n
        FWD[:, kc * 1024:kc * 1024 + 512] = np.cos(a)
        FWD[:, kc * 1024 + 512:kc * 1024 + 1024] = -np.sin(a)
    FNY = np.zeros((128, 4), dtype=np.float64)
    for kc in range(4):
        FNY[:, kc] = (-1.0) ** (np.arange(kc * 128, kc * 128 + 128))
    tfull = np.arange(1024)
    FINV = np.zeros((128, 8192), dtype=np.float64)
    for kc in range(8):
        rows = np.arange(kc * 128, (kc + 1) * 128)
        blk = np.zeros((128, 1024))
        for i, r in enumerate(rows):
            if r < 512:
                coef = 1.0 if r == 0 else 2.0
                blk[i] = coef * np.cos(2 * np.pi * tfull * r / n) / n
            else:
                fq = r - 512
                blk[i] = (-2.0 * np.sin(2 * np.pi * tfull * fq / n) / n
                          if fq >= 1 else 0.0)
        for tc8 in range(8):
            FINV[:, kc * 1024 + tc8 * 128:kc * 1024 + (tc8 + 1) * 128] = \
                blk[:, tc8 * 128:(tc8 + 1) * 128]
    FINVNY = ((-1.0) ** tfull / n).reshape(1, 1024)
    return dict(
        KTP=KTP.astype(np.float32),
        FWD=FWD.astype(np.float32), FNY=FNY.astype(np.float32),
        FINV=FINV.astype(np.float32), FINVNY=FINVNY.astype(np.float32),
        SKIP=np.ascontiguousarray(skip.reshape(1, 1024)),
    )


def emit_l1attn(sch, P, stack, dram_w, x_src, x_dst, bias_hbm, tag):
    nc = sch.nc
    CC = 512
    xch = sb(stack, nc, tag + "_x", [128, NCH * CC], F32)
    xn_hi = sb(stack, nc, tag + "_xnh", [128, NCH * CC], BF16)
    xn_lo = sb(stack, nc, tag + "_xnl", [128, NCH * CC], BF16)
    wqk_hi = sb(stack, nc, tag + "_wqh", [128, NCH * 128], BF16)
    wqk_lo = sb(stack, nc, tag + "_wql", [128, NCH * 128], BF16)
    wv_hi = sb(stack, nc, tag + "_wvh", [128, NCH * 1024], BF16)
    wv_lo = sb(stack, nc, tag + "_wvl", [128, NCH * 1024], BF16)
    wstage = sb(stack, nc, tag + "_wst", [128, 1024], F32)
    gb = sb(stack, nc, tag + "_gb", [128, 4], F32)
    bqk = sb(stack, nc, tag + "_bqk", [128, 1], F32)
    bvb = sb(stack, nc, tag + "_bvb", [128, 1024], F32)
    rel = sb(stack, nc, tag + "_rel", [32, 1], F32)
    grev = sb(stack, nc, tag + "_gr", [32, 512], F32)
    extrev = sb(stack, nc, tag + "_ext", [1, 1024], F32)
    biasT = [sb(stack, nc, tag + f"_bt{j}", [128, 512], F32) for j in range(4)]
    qkf = sb(stack, nc, tag + "_qkf", [128, CC], F32)
    q_hi = sb(stack, nc, tag + "_qh", [128, CC], BF16)
    q_lo = sb(stack, nc, tag + "_ql", [128, CC], BF16)
    k_hi = sb(stack, nc, tag + "_kh", [128, CC], BF16)
    k_lo = sb(stack, nc, tag + "_kl", [128, CC], BF16)
    v_hi = [sb(stack, nc, tag + f"_vh{j}", [128, 1025], BF16) for j in range(4)]
    v_lo = [sb(stack, nc, tag + f"_vl{j}", [128, 1025], BF16) for j in range(4)]
    E_hi = [sb(stack, nc, tag + f"_eh{j}", [128, 512], BF16) for j in range(4)]
    E_lo = [sb(stack, nc, tag + f"_el{j}", [128, 512], BF16) for j in range(4)]
    ao = [sb(stack, nc, tag + f"_ao{t}", [128, 1024], F32) for t in range(4)]
    tmp = [sb(stack, nc, tag + f"_t{i}", [128, 1024], F32) for i in range(2)]
    dinv = sb(stack, nc, tag + "_di", [128, 8], F32)
    stat = sb(stack, nc, tag + "_st", [1, 4 * CC], F32)
    pb = P.pb
    bankv = {}

    def twrite(b):
        sch.wait("tensor", "v", bankv.get(b, 0))
        return pb[b]

    sch.dma("sync", gb, dram_w["GB"], "ds")
    sch.dma("sync", bqk, dram_w["BQK"], "ds")
    sch.dma("sync", bvb, dram_w["BVB"], "ds")
    sch.dma("sync", rel, dram_w["REL"], "ds")
    ds_g = sch.dma("sync", grev, dram_w["GREV"], "ds")
    ds_wqk = sch.dma("sync", wstage[:, 0:1024].bitcast(F32), dram_w["WQK"].rearrange("(c p) n -> p (c n)", p=128), "ds") if False else None
    for ic in range(NCH):
        sch.dma("sync", wstage[:, ic * 128:(ic + 1) * 128],
                dram_w["WQK"][ic * 128:(ic + 1) * 128, :], "ds")
    ds_wqk = sch.cnt["ds"]
    sch.wait("vector", "ds", ds_wqk)
    sch.op("vector", lambda E: E.tensor_copy(wqk_hi, wstage), "v")
    v_w = sch.op("vector", lambda E: E.tensor_tensor(
        wqk_lo, wstage, wqk_hi, ALU.subtract), "v")
    for ic in range(NCH):
        sch.wait("sync", "v", v_w)
        ds_wv = sch.dma("sync", wstage, dram_w["WVT"][ic * 128:(ic + 1) * 128, :], "ds")
        sch.wait("vector", "ds", ds_wv)
        sch.op("vector", lambda E, ic=ic: E.tensor_copy(
            wv_hi[:, ic * 1024:(ic + 1) * 1024], wstage), "v")
        v_w = sch.op("vector", lambda E, ic=ic: E.tensor_tensor(
            wv_lo[:, ic * 1024:(ic + 1) * 1024], wstage,
            wv_hi[:, ic * 1024:(ic + 1) * 1024], ALU.subtract), "v")
    sch.wait("tensor", "ds", ds_g)
    bank7 = twrite(7)
    t_b = sch.op("tensor", lambda E: E.matmul(
        bank7[0:1, 0:512], rel, grev, start=True, stop=True), "t")
    sch.wait("scalar", "t", t_b)
    s_b = sch.op("scalar", lambda E: E.activation(
        extrev[0:1, 0:512], pb[7][0:1, 0:512], ACT.Copy,
        scale=float(128) ** 0.5), "s")
    sch.wait("vector", "s", s_b)
    bankv[7] = sch.op("vector", lambda E: E.memset(
        extrev[0:1, 512:1024], -1e30), "v")
    v_ext = sch.cnt["v"]
    sch.wait("scalar", "v", v_ext)
    da_ext = sch.dma("scalar", bias_hbm, extrev, "da")
    sch.wait("sync", "da", da_ext)
    for jc in range(4):
        sch.dma("sync", biasT[jc],
                bass.AP(bias_hbm.tensor, 128 * jc, [[1, 128], [1, 512]]),
                "ds")
    ds_hk = sch.cnt["ds"]
    sch.wait("vector", "ds", ds_hk)
    for jc in range(4):
        sch.op("vector", lambda E, jc=jc: E.tensor_copy(
            tmp[0][:, 0:512],
            bass.AP(biasT[jc].tensor, biasT[jc].offset + 511,
                    [[512, 128], [-1, 512]])))
        sch.op("vector", lambda E, jc=jc: E.tensor_copy(
            biasT[jc], tmp[0][:, 0:512]), "v")
    v_bias = sch.cnt["v"]

    da_prev = 0
    t_xread_prev = 0
    for cc in range(4):
        col0 = cc * CC
        sch.wait("sync", "da", da_prev)
        sch.wait("sync", "v", t_xread_prev)
        for c in range(NCH):
            sch.dma("sync", xch[:, c * CC:(c + 1) * CC],
                    x_src[c * 128:(c + 1) * 128, col0:col0 + CC], "ds")
        ds_x = sch.cnt["ds"]

        sch.wait("scalar", "ds", ds_x)
        if cc > 0:
            sch.wait("tensor", "v",
                     max(bankv.get(4, 0), bankv.get(5, 0)))
        t_sq = {}
        for c in range(NCH):
            if c >= 2:
                sch.wait("scalar", "t", t_sq[c - 2])
            s_sq = sch.op("scalar", lambda E, c=c: E.activation(
                tmp[c % 2][:, 0:CC], xch[:, c * CC:(c + 1) * CC],
                ACT.Square), "s")
            sch.wait("tensor", "s", s_sq)
            if c == 0:
                sch.wait("tensor", "ds", ds_x)
            sch.op("tensor", lambda E, c=c: E.matmul(
                pb[4][0:1, :], P.ones_col[:, 0:1],
                xch[:, c * CC:(c + 1) * CC],
                start=(c == 0), stop=(c == NCH - 1)),
                "t" if c == NCH - 1 else None)
            t_sq[c] = sch.op("tensor", lambda E, c=c: E.matmul(
                pb[5][0:1, :], P.ones_col[:, 0:1], tmp[c % 2][:, 0:CC],
                start=(c == 0), stop=(c == NCH - 1)), "t")
        t_stats = sch.cnt["t"]
        sum_x = stat[0:1, 0:CC]
        sum_sq = stat[0:1, CC:2 * CC]
        m_sb = stat[0:1, 2 * CC:3 * CC]
        rstd_sb = stat[0:1, 3 * CC:4 * CC]
        sch.wait("vector", "t", t_stats)
        sch.op("vector", lambda E: E.tensor_copy(sum_x, pb[4][0:1, :]))
        sch.op("vector", lambda E: E.tensor_copy(sum_sq, pb[5][0:1, :]))
        sch.op("vector", lambda E: E.tensor_scalar_mul(m_sb, sum_x, 1.0 / D))
        sch.op("vector", lambda E: E.tensor_scalar_mul(sum_sq, sum_sq, 1.0 / D))
        sch.op("vector", lambda E: E.tensor_tensor(sum_x, m_sb, m_sb, ALU.mult))
        sch.op("vector", lambda E: E.tensor_tensor(
            sum_sq, sum_sq, sum_x, ALU.subtract))
        v_var = sch.op("vector", lambda E: E.tensor_scalar_add(
            sum_sq, sum_sq, 1e-5), "v")
        sch.wait("scalar", "v", v_var)
        s_sd = sch.op("scalar", lambda E: E.activation(
            sum_x, sum_sq, ACT.Sqrt), "s")
        sch.wait("vector", "s", s_sd)
        v_rstd = sch.op("vector", lambda E: E.reciprocal(rstd_sb, sum_x), "v")
        sch.wait("tensor", "v", v_rstd)
        b4 = twrite(4)
        sch.op("tensor", lambda E, b4=b4: E.matmul(
            b4, P.ones_row, m_sb, start=True, stop=True))
        b5 = twrite(5)
        t_bc = sch.op("tensor", lambda E, b5=b5: E.matmul(
            b5, P.ones_row, rstd_sb, start=True, stop=True), "t")
        sch.wait("vector", "t", t_bc)
        for c in range(NCH):
            xs = xch[:, c * CC:(c + 1) * CC]
            hi = xn_hi[:, c * CC:(c + 1) * CC]
            lo = xn_lo[:, c * CC:(c + 1) * CC]
            t0 = tmp[c % 2][:, 0:CC]
            sch.op("vector", lambda E, xs=xs, t0=t0: E.tensor_tensor(
                t0, xs, pb[4], ALU.subtract))
            sch.op("vector", lambda E, t0=t0: E.tensor_tensor(
                t0, t0, pb[5], ALU.mult))
            sch.op("vector", lambda E, hi=hi, t0=t0: E.tensor_copy(hi, t0))
            sch.op("vector", lambda E, hi=hi, lo=lo, t0=t0: E.tensor_tensor(
                lo, t0, hi, ALU.subtract), "v")
        v_xn_done = sch.cnt["v"]
        bankv[4] = bankv[5] = v_xn_done

        sch.wait("tensor", "v", max(v_xn_done, v_w if cc == 0 else 0))
        b0 = twrite(0)
        n_mm = 0
        for wp, xp in ((wqk_hi, xn_hi), (wqk_hi, xn_lo), (wqk_lo, xn_hi)):
            for ic in range(NCH):
                n_mm += 1
                sch.op("tensor", lambda E, b0=b0, wp=wp, xp=xp, ic=ic,
                       st=(n_mm == 1), sp=(n_mm == 24): E.matmul(
                    b0, wp[:, ic * 128:(ic + 1) * 128],
                    xp[:, ic * CC:(ic + 1) * CC], start=st, stop=sp),
                    "t" if n_mm == 24 else None)
        t_qk = sch.cnt["t"]
        sch.wait("scalar", "t", t_qk)
        s_qk = sch.op("scalar", lambda E, b0=b0: E.activation(
            qkf, b0, ACT.Silu, bias=bqk), "s")
        sch.wait("vector", "s", s_qk)
        sch.op("vector", lambda E: E.tensor_scalar(
            tmp[0][:, 0:CC], qkf, gb[:, 0:1], gb[:, 2:3], ALU.mult, ALU.add))
        sch.op("vector", lambda E: E.tensor_copy(q_hi, tmp[0][:, 0:CC]))
        sch.op("vector", lambda E: E.tensor_tensor(
            q_lo, tmp[0][:, 0:CC], q_hi, ALU.subtract))
        sch.op("vector", lambda E: E.tensor_scalar(
            tmp[1][:, 0:CC], qkf, gb[:, 1:2], gb[:, 3:4], ALU.mult, ALU.add))
        sch.op("vector", lambda E: E.tensor_copy(k_hi, tmp[1][:, 0:CC]))
        bankv[0] = sch.op("vector", lambda E: E.tensor_tensor(
            k_lo, tmp[1][:, 0:CC], k_hi, ALU.subtract), "v")
        v_qk_done = sch.cnt["v"]

        for ts_ in range(4):
            for half in range(2):
                b = 1 + (half + 2 * ts_) % 2
                bank = twrite(b)
                n_mm = 0
                for xp, wp in ((xn_hi, wv_hi), (xn_lo, wv_hi), (xn_hi, wv_lo)):
                    for ic in range(NCH):
                        n_mm += 1
                        sch.op("tensor", lambda E, bank=bank, xp=xp, wp=wp,
                               ic=ic, ts_=ts_, half=half, st=(n_mm == 1),
                               sp=(n_mm == 24): E.matmul(
                            bank[:, 0:512],
                            xp[:, ic * CC + ts_ * 128:ic * CC + (ts_ + 1) * 128],
                            wp[:, ic * 1024 + half * 512:ic * 1024 + (half + 1) * 512],
                            start=st, stop=sp),
                            "t" if n_mm == 24 else None)
                t_v = sch.cnt["t"]
                sch.wait("vector", "t", t_v)
                sch.op("vector", lambda E, bank=bank, half=half: E.tensor_add(
                    tmp[0][:, 0:512], bank[:, 0:512],
                    bvb[:, half * 512:(half + 1) * 512]), "v")
                v_add = sch.cnt["v"]
                sch.wait("scalar", "v", v_add)
                s_v = sch.op("scalar", lambda E: E.activation(
                    tmp[1][:, 0:512], tmp[0][:, 0:512], ACT.Silu), "s")
                sch.wait("vector", "s", s_v)
                hi = v_hi[ts_][:, half * 512:(half + 1) * 512]
                lo = v_lo[ts_][:, half * 512:(half + 1) * 512]
                sch.op("vector", lambda E, hi=hi: E.tensor_copy(
                    hi, tmp[1][:, 0:512]))
                bankv[b] = sch.op("vector", lambda E, hi=hi, lo=lo:
                                  E.tensor_tensor(lo, tmp[1][:, 0:512], hi,
                                                  ALU.subtract), "v")
            sch.op("vector", lambda E, ts_=ts_: E.memset(
                v_hi[ts_][:, 1024:1025], 1.0))
            sch.op("vector", lambda E, ts_=ts_: E.memset(
                v_lo[ts_][:, 1024:1025], 0.0), "v")
        v_v_done = sch.cnt["v"]

        sch.wait("tensor", "v", max(v_qk_done, v_bias))
        for jc in range(4):
            b = jc % 2
            bank = twrite(b)
            n_mm = 0
            for kp, qp in ((k_hi, q_hi), (k_hi, q_lo), (k_lo, q_hi)):
                n_mm += 1
                sch.op("tensor", lambda E, bank=bank, kp=kp, qp=qp, jc=jc,
                       st=(n_mm == 1), sp=(n_mm == 3): E.matmul(
                    bank, kp[:, jc * 128:(jc + 1) * 128], qp,
                    start=st, stop=sp), "t" if n_mm == 3 else None)
            t_s = sch.cnt["t"]
            sch.wait("vector", "t", t_s)
            sch.op("vector", lambda E, bank=bank, jc=jc: E.scalar_tensor_tensor(
                tmp[0][:, 0:512], bank, 1.0 / 32.0, biasT[jc],
                ALU.mult, ALU.add), "v")
            v_e = sch.cnt["v"]
            sch.wait("scalar", "v", v_e)
            s_e = sch.op("scalar", lambda E: E.activation(
                tmp[1][:, 0:512], tmp[0][:, 0:512], ACT.Exp), "s")
            sch.wait("vector", "s", s_e)
            sch.op("vector", lambda E, jc=jc: E.tensor_copy(
                E_hi[jc], tmp[1][:, 0:512]))
            bankv[b] = sch.op("vector", lambda E, jc=jc: E.tensor_tensor(
                E_lo[jc], tmp[1][:, 0:512], E_hi[jc], ALU.subtract), "v")
        v_E_done = sch.cnt["v"]

        sch.wait("tensor", "v", max(v_E_done, v_v_done))
        for it in range(4):
            bden = twrite(6)
            n_mm = 0
            for ep, vp in ((E_hi, v_hi), (E_hi, v_lo), (E_lo, v_hi)):
                for jc in range(4):
                    n_mm += 1
                    sch.op("tensor", lambda E, bden=bden, ep=ep, vp=vp, jc=jc,
                           it=it, st=(n_mm == 1), sp=(n_mm == 12): E.matmul(
                        bden[:, 0:1], ep[jc][:, it * 128:(it + 1) * 128],
                        vp[jc][:, 1024:1025], start=st, stop=sp),
                        "t" if n_mm == 12 else None)
            t_d = sch.cnt["t"]
            sch.wait("vector", "t", t_d)
            sch.op("vector", lambda E, it=it: E.tensor_copy(
                dinv[:, it:it + 1], pb[6][:, 0:1]), "v")
            sch.selfwait("vector", "v")
            bankv[6] = sch.op("vector", lambda E, it=it: E.reciprocal(
                dinv[:, it:it + 1], dinv[:, it:it + 1]), "v")
            v_di = sch.cnt["v"]
            for half in range(2):
                b = 2 + (half + 2 * it) % 2
                bank = twrite(b)
                n_mm = 0
                for ep, vp in ((E_hi, v_hi), (E_hi, v_lo), (E_lo, v_hi)):
                    for jc in range(4):
                        n_mm += 1
                        sch.op("tensor", lambda E, bank=bank, ep=ep, vp=vp,
                               jc=jc, it=it, half=half, st=(n_mm == 1),
                               sp=(n_mm == 12): E.matmul(
                            bank[:, 0:512],
                            ep[jc][:, it * 128:(it + 1) * 128],
                            vp[jc][:, half * 512:(half + 1) * 512],
                            start=st, stop=sp), "t" if n_mm == 12 else None)
                t_av = sch.cnt["t"]
                sch.wait("vector", "t", t_av)
                sch.wait("vector", "v", v_di)
                bankv[b] = sch.op("vector", lambda E, bank=bank, it=it,
                                  half=half: E.tensor_scalar_mul(
                    ao[it][:, half * 512:(half + 1) * 512], bank[:, 0:512],
                    dinv[:, it:it + 1]), "v")
        v_ao_done = sch.cnt["v"]

        sch.wait("tensor", "v", v_ao_done)
        for it in range(4):
            for c in range(NCH):
                b = 4 + ((it * NCH + c) % 2)
                bank = twrite(b)
                sch.op("tensor", lambda E, bank=bank, it=it, c=c: E.transpose(
                    bank[:, 0:128], ao[it][:, c * 128:(c + 1) * 128],
                    P.ident), "t")
                t_tp = sch.cnt["t"]
                sch.wait("vector", "t", t_tp)
                dst = xch[:, c * CC + it * 128:c * CC + (it + 1) * 128]
                bankv[b] = sch.op("vector", lambda E, bank=bank, dst=dst:
                                  E.tensor_add(dst, dst, bank[:, 0:128]), "v")
        v_out = sch.cnt["v"]
        t_xread_prev = v_out
        sch.wait("scalar", "v", v_out)
        for c in range(NCH):
            da_prev = sch.dma(
                "scalar", x_dst[c * 128:(c + 1) * 128, col0:col0 + CC],
                xch[:, c * CC:(c + 1) * CC], "da")
    sch.wait("scalar", "da", sch.cnt["da"])


def prep_l1attn_inputs(Wqk, bqk, Wv, bv, gamma, beta, rel_emb):
    import math
    d = np.arange(512)
    n = np.maximum(511 - d, 0)
    max_exact = 16
    n_f = np.maximum(n, 1).astype(np.float32)
    val = (max_exact + np.log(n_f / np.float32(max_exact)) /
           np.float32(math.log(128 / 16)) * (32 - max_exact))
    val = np.minimum(val, np.float32(31))
    bucket = np.where(n < max_exact, n.astype(np.float32), val).astype(np.int32)
    GREV = np.zeros((32, 512), np.float32)
    GREV[bucket, np.arange(512)] = 1.0
    GB = np.ascontiguousarray(
        np.stack([gamma[0], gamma[1], beta[0], beta[1]], 1))
    return dict(
        WQK=np.ascontiguousarray(Wqk), WVT=np.ascontiguousarray(Wv),
        GB=GB, BQK=np.ascontiguousarray(bqk.reshape(128, 1)),
        BVB=np.ascontiguousarray(np.tile(bv, (128, 1))),
        REL=np.ascontiguousarray(rel_emb.reshape(32, 1)), GREV=GREV,
    )


def emit_l1ffn(sch, P, stack, dram_w, x_src, x_dst, lacc_hbm, tag):
    nc = sch.nc
    pb = P.pb
    bankv = {}

    def twrite(b):
        g = bankv.get(b, 0)
        if isinstance(g, tuple):
            sch.wait("tensor", g[0], g[1])
        else:
            sch.wait("tensor", "v", g)
        return pb[b]

    def tail_pad():
        return sch.op("tensor", lambda E: E.matmul(
            pb[7][0:1, 0:128], P.ones_row[0:1, 0:1], P.ones_row,
            start=True, stop=True), "t")

    xblk = sb(stack, nc, tag + "_xb", [128, NCH * 512], F32)
    wides = sb(stack, nc, tag + "_wd", [1, 2 * NT], F32)
    rout = sb(stack, nc, tag + "_ro", [128, NT], F32)
    ERR = rout[0:2, :]
    SCO = rout[32:34, :]
    ZR = rout[96:98, :]
    ab = sb(stack, nc, tag + "_abr", [8, 4], F32)[0:2, :]
    logk = sb(stack, nc, tag + "_lgk", [2, 1], F32)
    rt = sb(stack, nc, tag + "_rt", [128, NCH * 2], F32)
    gl = sb(stack, nc, tag + "_gl", [128, NCH], F32)
    gh = sb(stack, nc, tag + "_ghg", [128, NCH], F32)
    stat = sb(stack, nc, tag + "_st", [1, 4 * 512], F32)
    ckvT = sb(stack, nc, tag + "_ckt", [128, 16], F32)
    cqmT = sb(stack, nc, tag + "_cqt", [128, 16], F32)
    scqT = sb(stack, nc, tag + "_sqt", [128, 16], F32)
    jT = sb(stack, nc, tag + "_jt", [128, 16], F32)
    hsel = sb(stack, nc, tag + "_hsl", [128, 32], F32)
    mh = sb(stack, nc, tag + "_mh", [128, 24], F32)
    negMh = sb(stack, nc, tag + "_nmh", [128, 8], F32)
    tmp = [sb(stack, nc, tag + f"_t{i}", [128, 2048], F32) for i in range(2)]
    nms = sb(stack, nc, tag + "_nms", [8, 512], F32)
    cqrow = rout[64:65, :]
    maskr = stat

    for c in range(NCH):
        sch.dma("sync", rt[:, 2 * c:2 * c + 2],
                dram_w["RT"][c * 128:(c + 1) * 128, :], "ds")
    sch.dma("sync", gl, dram_w["GL"], "ds")
    sch.dma("sync", gh, dram_w["GH"], "ds")
    sch.dma("sync", logk, dram_w["LOGK"], "ds")
    ds_small = sch.dma("sync", hsel, dram_w["HSEL"], "ds")
    sch.dma("sync", jT, dram_w["JT"], "ds")

    v_xblk_free = 0
    s_xblk_free = 0
    score_t = {}
    for tb in range(4):
        col0 = tb * 512
        sch.wait("sync", "v", v_xblk_free)
        sch.wait("sync", "s", s_xblk_free)
        for c in range(NCH):
            sch.dma("sync", xblk[:, c * 512:(c + 1) * 512],
                    x_src[c * 128:(c + 1) * 128, col0:col0 + 512], "ds")
        ds_x = sch.cnt["ds"]
        sch.wait("scalar", "ds", ds_x)
        sch.wait("tensor", "v", max(bankv.get(4, 0), bankv.get(5, 0)))
        t_sq = {}
        for c in range(NCH):
            if c >= 2:
                sch.wait("scalar", "t", t_sq[c - 2])
            elif score_t:
                sch.wait("scalar", "t", score_t[c % 2])
            s_sq = sch.op("scalar", lambda E, c=c: E.activation(
                tmp[c % 2][:, 0:512], xblk[:, c * 512:(c + 1) * 512],
                ACT.Square), "s")
            sch.wait("tensor", "s", s_sq)
            if c == 0:
                sch.wait("tensor", "ds", ds_x)
            sch.op("tensor", lambda E, c=c: E.matmul(
                pb[4][0:1, :], P.ones_col[:, 0:1],
                xblk[:, c * 512:(c + 1) * 512],
                start=(c == 0), stop=(c == NCH - 1)),
                "t" if c == NCH - 1 else None)
            t_sq[c] = sch.op("tensor", lambda E, c=c: E.matmul(
                pb[5][0:1, :], P.ones_col[:, 0:1], tmp[c % 2][:, 0:512],
                start=(c == 0), stop=(c == NCH - 1)), "t")
        t_stats = sch.cnt["t"]
        s_xblk_free = sch.cnt["s"]
        sum_x = stat[0:1, 0:512]
        sum_sq = stat[0:1, 512:1024]
        m_sb = stat[0:1, 1024:1536]
        rstd_sb = stat[0:1, 1536:2048]
        sch.wait("vector", "t", t_stats)
        sch.op("vector", lambda E: E.tensor_copy(sum_x, pb[4][0:1, :]))
        sch.op("vector", lambda E: E.tensor_copy(sum_sq, pb[5][0:1, :]))
        sch.op("vector", lambda E: E.tensor_scalar_mul(m_sb, sum_x, 1.0 / D))
        sch.op("vector", lambda E: E.tensor_scalar_mul(sum_sq, sum_sq, 1.0 / D))
        sch.op("vector", lambda E: E.tensor_tensor(sum_x, m_sb, m_sb, ALU.mult))
        sch.op("vector", lambda E: E.tensor_tensor(
            sum_sq, sum_sq, sum_x, ALU.subtract), "v")
        v_var = sch.op("vector", lambda E: E.tensor_scalar_add(
            sum_x, sum_sq, 1e-5), "v")
        sch.wait("scalar", "v", v_var)
        s_sd = sch.op("scalar", lambda E: E.activation(
            sum_x, sum_x, ACT.Sqrt), "s")
        sch.wait("vector", "s", s_sd)
        sch.op("vector", lambda E: E.reciprocal(rstd_sb, sum_x), "v")
        sch.selfwait("vector", "v")
        sch.op("vector", lambda E: E.tensor_scalar_mul(
            sum_x, sum_sq, 1024.0), "v")
        sch.op("vector", lambda E: E.tensor_tensor(
            sum_sq, rstd_sb, rstd_sb, ALU.mult), "v")
        sch.selfwait("vector", "v")
        v_n2 = sch.op("vector", lambda E: E.tensor_tensor(
            sum_x, sum_x, sum_sq, ALU.mult), "v")
        sch.wait("scalar", "v", v_n2)
        s_n = sch.op("scalar", lambda E: E.activation(
            sum_x, sum_x, ACT.Sqrt), "s")
        sch.wait("vector", "s", s_n)
        sch.op("vector", lambda E: E.tensor_scalar_max(
            sum_x, sum_x, 1e-12), "v")
        sch.selfwait("vector", "v")
        sch.op("vector", lambda E: E.reciprocal(sum_sq, sum_x), "v")
        sch.selfwait("vector", "v")
        sch.op("vector", lambda E: E.tensor_scalar_mul(
            sum_sq, sum_sq, 32.0), "v")
        sch.selfwait("vector", "v")
        sch.op("vector", lambda E, col0=col0: E.tensor_tensor(
            wides[0:1, NT + col0:NT + col0 + 512], rstd_sb, sum_sq,
            ALU.mult), "v")
        v_wide = sch.op("vector", lambda E, col0=col0: E.tensor_copy(
            wides[0:1, col0:col0 + 512], m_sb), "v")
        sch.wait("tensor", "v", v_wide)
        b4 = twrite(4)
        sch.op("tensor", lambda E, b4=b4: E.matmul(
            b4, P.ones_row, m_sb, start=True, stop=True))
        b5 = twrite(5)
        t_bc = sch.op("tensor", lambda E, b5=b5: E.matmul(
            b5, P.ones_row, rstd_sb, start=True, stop=True), "t")
        sch.wait("vector", "t", t_bc)
        for c in range(NCH):
            xs = xblk[:, c * 512:(c + 1) * 512]
            t0 = tmp[c % 2][:, 0:512]
            if c >= 2:
                sch.wait("vector", "t", score_t[c % 2])
            sch.op("vector", lambda E, xs=xs, t0=t0: E.tensor_tensor(
                t0, xs, pb[4], ALU.subtract))
            v_x = sch.op("vector", lambda E, t0=t0: E.tensor_tensor(
                t0, t0, pb[5], ALU.mult), "v")
            sch.wait("tensor", "v", v_x)
            if c == 0:
                sch.wait("tensor", "ds", ds_small)
                twrite(6)
            score_t[c % 2] = sch.op("tensor", lambda E, c=c, t0=t0: E.matmul(
                pb[6][0:2, 0:512], rt[:, 2 * c:2 * c + 2], t0,
                start=(c == 0), stop=(c == NCH - 1)), "t")
        t_s = sch.cnt["t"]
        sch.wait("scalar", "t", t_s)
        bankv[6] = ("s", sch.op("scalar", lambda E, col0=col0: E.activation(
            SCO[0:2, col0:col0 + 512], pb[6][0:2, 0:512], ACT.Copy), "s"))
        bankv[4] = bankv[5] = sch.cnt["v"]
        v_xblk_free = sch.cnt["v"]
    v_sco = sch.cnt["v"]
    s_sco = sch.cnt["s"]

    g_sem = "g"
    sch.sem(g_sem)

    def emit_chain_prefix():
        sch.wait("gpsimd", "v", v_sco)
        sch.wait("gpsimd", "s", s_sco)
        sch.wait("gpsimd", "ds", ds_small)
        sch.op("gpsimd", lambda E: E.memset(ab[0:2, 0:1], 0.0), g_sem)

    def emit_chain_iters(n):
        for _ in range(n):
            sch.selfwait("gpsimd", g_sem)
            sch.op("gpsimd", lambda E: E.tensor_scalar(
                ZR, SCO, ab[0:2, 0:1], 0.0, ALU.add, ALU.min), g_sem)
            g_z = sch.cnt[g_sem]
            sch.wait("scalar", g_sem, g_z)
            s_e = sch.op("scalar", lambda E: E.activation(ERR, ZR, ACT.Exp),
                         "s")
            sch.wait("vector", "s", s_e)
            v_r = sch.op("vector", lambda E: E.reduce_sum(
                ab[0:2, 1:2], ERR, AX.X), "v")
            sch.wait("scalar", "v", v_r)
            s_l = sch.op("scalar", lambda E: E.activation(
                ab[0:2, 2:3], ab[0:2, 1:2], ACT.Ln), "s")
            sch.wait("gpsimd", "s", s_l)
            sch.op("gpsimd", lambda E: E.tensor_scalar(
                ab[0:2, 0:1], ab[0:2, 2:3], logk[0:2, 0:1], -1.0,
                ALU.subtract, ALU.mult), g_sem)

    def emit_chain_final():
        sch.selfwait("gpsimd", g_sem)
        sch.op("gpsimd", lambda E: E.tensor_scalar(
            ZR, SCO, ab[0:2, 0:1], 0.0, ALU.add, ALU.min), g_sem)
        g_z = sch.cnt[g_sem]
        sch.wait("scalar", g_sem, g_z)
        return sch.op("scalar", lambda E: E.activation(ERR, ZR, ACT.Exp), "s")

    def emit_negM(qh, ql, kh, kl, ocspan, ncb, scaleval):
        for (sh, sl_, col) in ((qh, ql, 16), (kh, kl, 20)):
            for cb in range(ncb):
                b6 = twrite(6)
                for oc in range(4):
                    sch.op("vector", lambda E, sh=sh, sl_=sl_, oc=oc, cb=cb:
                           E.tensor_tensor(
                        tmp[0][:, 0:512],
                        sh[:, oc * ocspan + cb * 512:oc * ocspan + (cb + 1) * 512],
                        sl_[:, oc * ocspan + cb * 512:oc * ocspan + (cb + 1) * 512],
                        ALU.add))
                    v_t = sch.op("vector", lambda E: E.tensor_tensor(
                        tmp[0][:, 0:512], tmp[0][:, 0:512], tmp[0][:, 0:512],
                        ALU.mult), "v")
                    sch.wait("tensor", "v", v_t)
                    sch.op("tensor", lambda E, b6=b6, oc=oc: E.matmul(
                        b6[0:8, 0:512], hsel[:, oc * 8:(oc + 1) * 8],
                        tmp[0][:, 0:512], start=(oc == 0), stop=(oc == 3)),
                        "t" if oc == 3 else None)
                t_n = sch.cnt["t"]
                sch.wait("vector", "t", t_n)
                sch.op("vector", lambda E: E.tensor_copy(
                    nms, pb[6][0:8, 0:512]), "v")
                sch.selfwait("vector", "v")
                bankv[6] = sch.op("vector", lambda E, cb=cb, col=col:
                                  E.reduce_max(mh[0:8, col + cb:col + cb + 1],
                                               nms, AX.X), "v")
            sch.selfwait("vector", "v")
            sch.op("vector", lambda E, col=col, ncb=ncb: E.reduce_max(
                mh[0:8, col - 16:col - 15], mh[0:8, col:col + ncb], AX.X),
                "v")
        sch.selfwait("vector", "v")
        v_m2 = sch.op("vector", lambda E: E.tensor_tensor(
            mh[0:8, 2:3], mh[0:8, 0:1], mh[0:8, 4:5], ALU.mult), "v")
        sch.wait("scalar", "v", v_m2)
        s_m = sch.op("scalar", lambda E: E.activation(
            mh[0:8, 3:4], mh[0:8, 2:3], ACT.Sqrt), "s")
        sch.wait("vector", "s", s_m)
        v_m = sch.op("vector", lambda E, scaleval=scaleval: E.tensor_scalar_mul(
            mh[0:8, 6:7], mh[0:8, 3:4], scaleval), "v")
        sch.wait("tensor", "v", v_m)
        b7 = twrite(7)
        sch.op("tensor", lambda E, b7=b7: E.transpose(
            b7[0:1, 0:8], mh[0:8, 6:7], P.ident[0:8, 0:8]), "t")
        t_tr = sch.cnt["t"]
        sch.wait("scalar", "t", t_tr)
        bankv[7] = ("s", sch.op("scalar", lambda E: E.activation(
            mh[0:1, 8:16], pb[7][0:1, 0:8], ACT.Copy), "s"))
        sch.wait("tensor", "s", bankv[7][1])
        b7 = twrite(7)
        sch.op("tensor", lambda E, b7=b7: E.matmul(
            b7[:, 0:8], P.ones_row, mh[0:1, 8:16], start=True, stop=True),
            "t")
        t_b = sch.cnt["t"]
        sch.wait("scalar", "t", t_b)
        bankv[7] = ("s", sch.op("scalar", lambda E: E.activation(
            negMh, pb[7][:, 0:8], ACT.Copy), "s"))
        sch.wait("vector", "s", bankv[7][1])
        return sch.op("vector", lambda E: E.engine_nop(), "v")

    with ExitStack() as lstack:
        wstg = sb(lstack, nc, tag + "_lws", [128, 1024], F32)
        ql_hi = sb(lstack, nc, tag + "_qlh", [128, 4 * 512], BF16)
        ql_lo = sb(lstack, nc, tag + "_qll", [128, 4 * 512], BF16)
        wql_hi = sb(lstack, nc, tag + "_lqh", [128, NCH * 512], BF16)
        wql_lo = sb(lstack, nc, tag + "_lql", [128, NCH * 512], BF16)
        wkl_hi = sb(lstack, nc, tag + "_lkh", [128, NCH * 1024], BF16)
        wkl_lo = sb(lstack, nc, tag + "_lkl", [128, NCH * 1024], BF16)
        wol_hi = sb(lstack, nc, tag + "_loh", [128, 4 * 1024], BF16)
        wol_lo = sb(lstack, nc, tag + "_lol", [128, 4 * 1024], BF16)
        xl_hi = sb(lstack, nc, tag + "_xlh", [128, NCH * 512], BF16)
        xl_lo = sb(lstack, nc, tag + "_xll", [128, NCH * 512], BF16)
        kl_hi = sb(lstack, nc, tag + "_klh", [128, 4 * 512], BF16)
        kl_lo = sb(lstack, nc, tag + "_kll", [128, 4 * 512], BF16)
        vl_f = sb(lstack, nc, tag + "_vlf", [128, 4 * 512], F32)
        vl_hi = sb(lstack, nc, tag + "_vlh", [128, 4 * 520], BF16)
        vl_lo = sb(lstack, nc, tag + "_vll", [128, 4 * 520], BF16)
        El_hi = sb(lstack, nc, tag + "_Elh", [128, 128], BF16)
        El_lo = sb(lstack, nc, tag + "_Ell", [128, 128], BF16)
        aoT_hi = sb(lstack, nc, tag + "_alh", [128, 4 * 512], BF16)
        aoT_lo = sb(lstack, nc, tag + "_all", [128, 4 * 512], BF16)
        facr = sb(lstack, nc, tag + "_lfc", [1, 512], F32)
        mql = sb(lstack, nc, tag + "_mql", [8, 512], F32)
        sthi = sb(lstack, nc, tag + "_sth", [128, 128], BF16)
        stlo = sb(lstack, nc, tag + "_stl", [128, 128], BF16)

        v_w = 0
        sch.fence("sync")
        for (wth, wtl, nmsrc, wid) in ((wql_hi, wql_lo, "WQL", 512),
                                       (wkl_hi, wkl_lo, "WKVL", 1024),
                                       (wol_hi, wol_lo, "WOL", 1024)):
            nchunks = NCH if nmsrc != "WOL" else 4
            for ic in range(nchunks):
                if v_w:
                    sch.wait("sync", "v", v_w)
                dsw = sch.dma("sync", wstg[:, 0:wid],
                              dram_w[nmsrc][ic * 128:(ic + 1) * 128, :], "ds")
                sch.wait("vector", "ds", dsw)
                sch.op("vector", lambda E, ic=ic, wth=wth, wid=wid,
                       wstg=wstg: E.tensor_copy(
                    wth[:, ic * wid:(ic + 1) * wid], wstg[:, 0:wid]), "v")
                v_w = sch.op("vector", lambda E, ic=ic, wth=wth, wtl=wtl,
                             wid=wid, wstg=wstg: E.tensor_tensor(
                    wtl[:, ic * wid:(ic + 1) * wid], wstg[:, 0:wid],
                    wth[:, ic * wid:(ic + 1) * wid], ALU.subtract), "v")

        if "DBG_WQL" in dram_w:
            sch.wait("gpsimd", "v", v_w)
            sch.dma("gpsimd", dram_w["DBG_WQL"], wql_hi, "dg")
            sch.dma("gpsimd", dram_w["DBG_WQLL"], wql_lo, "dg")
        if "DBG_WOL0" in dram_w:
            sch.wait("sync", "v", v_w)
            sch.dma("sync", dram_w["DBG_WOL0"], wol_hi, "ds")
            sch.dma("sync", dram_w["DBG_WOL0L"], wol_lo, "ds")
        emit_chain_prefix()
        da_l = 0
        lsplit_hist = [0, 0]
        for tb in range(4):
            col0 = tb * 512
            sch.wait("sync", "v", v_xblk_free)
            sch.wait("sync", "s", s_xblk_free)
            for c in range(NCH):
                sch.dma("sync", xblk[:, c * 512:(c + 1) * 512],
                        x_src[c * 128:(c + 1) * 128, col0:col0 + 512], "ds")
            ds_x = sch.cnt["ds"]
            sch.wait("tensor", "ds", ds_x)
            sch.wait("tensor", "v", max(bankv.get(4, 0), bankv.get(5, 0)))
            b4 = twrite(4)
            sch.op("tensor", lambda E, b4=b4, col0=col0: E.matmul(
                b4, P.ones_row, wides[0:1, col0:col0 + 512],
                start=True, stop=True))
            b5 = twrite(5)
            t_bc = sch.op("tensor", lambda E, b5=b5, col0=col0: E.matmul(
                b5, P.ones_row, wides[0:1, NT + col0:NT + col0 + 512],
                start=True, stop=True), "t")
            sch.wait("vector", "t", t_bc)
            sch.wait("vector", "da", da_l)
            for c in range(NCH):
                xs = xblk[:, c * 512:(c + 1) * 512]
                t0 = tmp[c % 2][:, 0:512]
                hi = xl_hi[:, c * 512:(c + 1) * 512]
                lo = xl_lo[:, c * 512:(c + 1) * 512]
                sch.op("vector", lambda E, xs=xs, t0=t0: E.tensor_tensor(
                    t0, xs, pb[4], ALU.subtract))
                sch.op("vector", lambda E, t0=t0: E.tensor_tensor(
                    t0, t0, pb[5], ALU.mult))
                sch.op("vector", lambda E, t0=t0, c=c: E.tensor_scalar_mul(
                    t0, t0, gl[:, c:c + 1]))
                sch.op("vector", lambda E, hi=hi, t0=t0: E.tensor_copy(hi, t0))
                sch.op("vector", lambda E, hi=hi, lo=lo, t0=t0:
                       E.tensor_tensor(lo, t0, hi, ALU.subtract), "v")
            v_xl = sch.cnt["v"]
            bankv[4] = bankv[5] = v_xl
            v_xblk_free = v_xl
            s_xblk_free = sch.cnt["s"]
            if "DBG_XL" in dram_w and tb == 3:
                sch.wait("gpsimd", "v", v_xl)
                sch.dma("gpsimd", dram_w["DBG_XL"], xl_hi, "dg")

            sch.wait("tensor", "v", v_xl)
            for (dh, dl, wh, wl, wcols) in (
                    (kl_hi, kl_lo, wkl_hi, wkl_lo, 1024),
                    (ql_hi, ql_lo, wql_hi, wql_lo, 512)):
                for oc in range(4):
                    b = oc % 2
                    bank = twrite(b)
                    n_mm = 0
                    for wp, xp in ((wh, xl_hi), (wh, xl_lo), (wl, xl_hi)):
                        for ic in range(NCH):
                            n_mm += 1
                            sch.op("tensor", lambda E, bank=bank, wp=wp, xp=xp,
                                   ic=ic, oc=oc, wcols=wcols, st=(n_mm == 1),
                                   sp=(n_mm == 24): E.matmul(
                                bank,
                                wp[:, ic * wcols + oc * 128:ic * wcols + (oc + 1) * 128],
                                xp[:, ic * 512:(ic + 1) * 512],
                                start=st, stop=sp), "t" if n_mm == 24 else None)
                    t_g = sch.cnt["t"]
                    sch.wait("vector", "t", t_g)
                    hi = dh[:, oc * 512:(oc + 1) * 512]
                    lo = dl[:, oc * 512:(oc + 1) * 512]
                    sch.op("vector", lambda E, hi=hi, bank=bank: E.tensor_copy(
                        hi, bank))
                    bankv[b] = sch.op("vector", lambda E, hi=hi, lo=lo,
                                      bank=bank: E.tensor_tensor(
                        lo, bank, hi, ALU.subtract), "v")
            for ts_ in range(4):
                b = 2 + (ts_ % 2)
                bank = twrite(b)
                n_mm = 0
                for xp, wp in ((xl_hi, wkl_hi), (xl_lo, wkl_hi),
                               (xl_hi, wkl_lo)):
                    for ic in range(NCH):
                        n_mm += 1
                        sch.op("tensor", lambda E, bank=bank, xp=xp, wp=wp,
                               ic=ic, ts_=ts_, st=(n_mm == 1),
                               sp=(n_mm == 24): E.matmul(
                            bank,
                            xp[:, ic * 512 + ts_ * 128:ic * 512 + (ts_ + 1) * 128],
                            wp[:, ic * 1024 + 512:(ic + 1) * 1024],
                            start=st, stop=sp), "t" if n_mm == 24 else None)
                t_g = sch.cnt["t"]
                sch.wait("vector", "t", t_g)
                if "DBG_VL" in dram_w:
                    sch.op("vector", lambda E, bank=bank, ts_=ts_:
                           E.tensor_copy(
                        vl_f[:, ts_ * 512:(ts_ + 1) * 512], bank))
                for h in range(8):
                    hi = vl_hi[:, ts_ * 520 + h * 65:ts_ * 520 + h * 65 + 64]
                    lo = vl_lo[:, ts_ * 520 + h * 65:ts_ * 520 + h * 65 + 64]
                    srcv = bank[:, h * 64:(h + 1) * 64]
                    sch.op("vector", lambda E, hi=hi, srcv=srcv: E.tensor_copy(
                        hi, srcv))
                    sch.op("vector", lambda E, hi=hi, lo=lo, srcv=srcv:
                           E.tensor_tensor(lo, srcv, hi, ALU.subtract), "v")
                bankv[b] = sch.cnt["v"]
                sch.op("vector", lambda E, ts_=ts_: E.memset(
                    bass.AP(vl_hi.tensor, vl_hi.offset + ts_ * 520 + 64,
                            [[4 * 520, 128], [65, 8]]), 1.0))
                sch.op("vector", lambda E, ts_=ts_: E.memset(
                    bass.AP(vl_lo.tensor, vl_lo.offset + ts_ * 520 + 64,
                            [[4 * 520, 128], [65, 8]]), 0.0), "v")
            v_vl = sch.cnt["v"]
            if "DBG_VL" in dram_w and tb == 3:
                sch.wait("gpsimd", "v", v_vl)
                sch.dma("gpsimd", dram_w["DBG_VL"], vl_f, "dg")
                sch.dma("gpsimd", dram_w["DBG_QL"], ql_hi, "dg")
                if "DBG_KL" in dram_w:
                    sch.dma("gpsimd", dram_w["DBG_KL"], kl_hi, "dg")
                    sch.dma("gpsimd", dram_w["DBG_QLL"], ql_lo, "dg")

            v_nm = emit_negM(ql_hi, ql_lo, kl_hi, kl_lo, 512, 1, -0.125)
            sch.selfwait("vector", "v")

            sch.wait("tensor", "v", max(v_vl, v_nm))
            sch.wait("scalar", "v", max(v_vl, v_nm))
            sch.wait("scalar", "da", da_l)
            for wi in range(4):
                for h in range(8):
                    hc, po = h // 2, (h % 2) * 64
                    b = (wi + h) % 2
                    bank = twrite(b)
                    n_mm = 0
                    for kp, qp in ((kl_hi, ql_hi), (kl_hi, ql_lo),
                                   (kl_lo, ql_hi)):
                        n_mm += 1
                        sch.op("tensor", lambda E, bank=bank, kp=kp, qp=qp,
                               wi=wi, hc=hc, po=po, st=(n_mm == 1),
                               sp=(n_mm == 3): E.matmul(
                            bank[:, 0:128],
                            kp[po:po + 64, hc * 512 + wi * 128:hc * 512 + (wi + 1) * 128],
                            qp[po:po + 64, hc * 512 + wi * 128:hc * 512 + (wi + 1) * 128],
                            start=st, stop=sp), "t" if n_mm == 3 else None)
                    t_si = sch.cnt["t"]
                    sch.wait("scalar", "t", t_si)
                    eb = tmp[1][:, ((wi + h) % 2) * 128:
                                ((wi + h) % 2) * 128 + 128]
                    sch.wait("scalar", "v", lsplit_hist[(wi + h) % 2])
                    s_ex = sch.op("scalar", lambda E, bank=bank, h=h, eb=eb:
                                  E.activation(
                        eb, bank[:, 0:128], ACT.Exp,
                        bias=negMh[:, h:h + 1], scale=0.125), "s")
                    bankv[b] = ("s", s_ex)
                    sch.wait("vector", "s", s_ex)
                    sch.op("vector", lambda E, eb=eb: E.tensor_copy(
                        El_hi, eb))
                    v_E = sch.op("vector", lambda E, eb=eb: E.tensor_tensor(
                        El_lo, eb, El_hi, ALU.subtract), "v")
                    lsplit_hist[(wi + h) % 2] = v_E
                    sch.wait("tensor", "v", v_E)
                    b2 = 2 + ((wi + h) % 2)
                    bank2 = twrite(b2)
                    n_mm = 0
                    for vp, ep in ((vl_hi, El_hi), (vl_hi, El_lo),
                                   (vl_lo, El_hi)):
                        n_mm += 1
                        sch.op("tensor", lambda E, bank2=bank2, vp=vp, ep=ep,
                               wi=wi, h=h, st=(n_mm == 1), sp=(n_mm == 3):
                               E.matmul(
                            bank2[0:65, 0:128],
                            vp[:, wi * 520 + h * 65:wi * 520 + (h + 1) * 65],
                            ep, start=st, stop=sp))
                    t_av = tail_pad()
                    sch.wait("vector", "t", t_av)
                    avs = tmp[1][0:65, 640:768]
                    bankv[b2] = sch.op(
                        "vector", lambda E, bank2=bank2, avs=avs:
                        E.tensor_copy(avs, bank2[0:65, 0:128]), "v")
                    sch.selfwait("vector", "v")
                    sch.op("vector", lambda E, avs=avs, facr=facr:
                           E.reciprocal(facr[0:1, 0:128],
                                        avs[64:65, :]), "v")
                    v_f = sch.cnt["v"]
                    sch.wait("tensor", "v", v_f)
                    b6 = twrite(6)
                    sch.op("tensor", lambda E, b6=b6, facr=facr: E.matmul(
                        b6[0:64, 0:128], P.ones_row[0:1, 0:64],
                        facr[0:1, 0:128], start=True, stop=True), "t")
                    t_fb = sch.cnt["t"]
                    sch.wait("vector", "t", t_fb)
                    sch.op("vector", lambda E: E.tensor_copy(
                        tmp[1][0:64, 896:1024], pb[6][0:64, 0:128]), "v")
                    sch.selfwait("vector", "v")
                    sch.op("vector", lambda E, avs=avs: E.tensor_tensor(
                        tmp[0][0:64, 0:128], avs[0:64, :],
                        tmp[1][0:64, 896:1024], ALU.mult), "v")
                    sch.selfwait("vector", "v")
                    dsth = aoT_hi[po:po + 64,
                                  hc * 512 + wi * 128:hc * 512 + (wi + 1) * 128]
                    dstl = aoT_lo[po:po + 64,
                                  hc * 512 + wi * 128:hc * 512 + (wi + 1) * 128]
                    sch.op("vector", lambda E: E.tensor_copy(
                        sthi[0:64, 0:128], tmp[0][0:64, 0:128]))
                    sch.op("vector", lambda E: E.tensor_tensor(
                        stlo[0:64, 0:128], tmp[0][0:64, 0:128],
                        sthi[0:64, 0:128], ALU.subtract), "v")
                    sch.selfwait("vector", "v")
                    sch.op("vector", lambda E, dsth=dsth: E.tensor_copy(
                        dsth, sthi[0:64, 0:128]))
                    bankv[6] = sch.op(
                        "vector", lambda E, dstl=dstl: E.tensor_copy(
                            dstl, stlo[0:64, 0:128]), "v")
            v_ao = sch.cnt["v"]
            sch.wait("tensor", "v", v_ao)
            for dc in range(NCH):
                b = 4 + (dc % 2)
                bank = twrite(b)
                n_mm = 0
                for wp, ap_ in ((wol_hi, aoT_hi), (wol_hi, aoT_lo),
                                (wol_lo, aoT_hi)):
                    for hc in range(4):
                        n_mm += 1
                        sch.op("tensor", lambda E, bank=bank, wp=wp, ap_=ap_,
                               hc=hc, dc=dc, st=(n_mm == 1), sp=(n_mm == 12):
                               E.matmul(
                            bank,
                            wp[:, hc * 1024 + dc * 128:hc * 1024 + (dc + 1) * 128],
                            ap_[:, hc * 512:(hc + 1) * 512],
                            start=st, stop=sp), "t" if n_mm == 12 else None)
                t_wo = sch.cnt["t"]
                sch.wait("vector", "t", t_wo)
                if da_l:
                    sch.wait("vector", "da", da_l)
                bankv[b] = sch.op("vector", lambda E, bank=bank, dc=dc:
                                  E.tensor_copy(tmp[dc % 2][:, 0:512], bank),
                                  "v")
                v_l = sch.cnt["v"]
                sch.wait("scalar", "v", v_l)
                da_l = sch.dma(
                    "scalar", lacc_hbm[dc * 128:(dc + 1) * 128, col0:col0 + 512],
                    tmp[dc % 2][:, 0:512], "da")
            if not DBG_CHAIN_DEFER:
                emit_chain_iters(13 if tb < 2 else 12)
        if DBG_CHAIN_DEFER:
            emit_chain_iters(50)
        if "DBG_WOL0" in dram_w:
            sch.wait("sync", "v", sch.cnt["v"])
            sch.dma("sync", dram_w["DBG_WOL1"], wol_hi, "ds")
            sch.dma("sync", dram_w["DBG_WOL1L"], wol_lo, "ds")
            sch.dma("sync", dram_w["DBG_AOTL"], aoT_hi, "ds")
            sch.dma("sync", dram_w["DBG_AOTLL"], aoT_lo, "ds")
        s_fin = emit_chain_final()
        sch.dbg_light = dict(vl_f=vl_f, facr=facr, aoT_hi=aoT_hi, ql_hi=ql_hi,
                             kl_hi=kl_hi, xl_hi=xl_hi, negMh_l=negMh)

    CV = tmp[1][0:2, 0:NT]
    sch.wait("vector", "s", s_fin)
    sch.op("vector", lambda E: E.tensor_scalar(
        tmp[0][0:2, 0:NT], ERR, -1.0, 1.0, ALU.mult, ALU.add), "v")
    sch.selfwait("vector", "v")
    v_cv = sch.op("vector", lambda E: E.tensor_add(
        CV, tmp[0][0:2, 0:NT], ERR), "v")

    def pair_to_cols(srcpair, dstT0, dstT1, v_ready, pbase):
        idn = P.ident[pbase:pbase + 2, pbase:pbase + 2]
        sch.wait("tensor", "v", v_ready)
        for tc in range(16):
            b = tc % 2
            bank = twrite(b)
            sch.op("tensor", lambda E, bank=bank, tc=tc, srcpair=srcpair,
                   idn=idn: E.transpose(bank[0:128, 0:2],
                                        srcpair[:, tc * 128:(tc + 1) * 128],
                                        idn), "t")
            t_t = sch.cnt["t"]
            sch.wait("scalar", "t", t_t)
            dsts = [(d, i) for d, i in ((dstT0, 0), (dstT1, 1))
                    if d is not None]
            for k, (dstT, ci) in enumerate(dsts):
                sch.op("scalar", lambda E, dstT=dstT, tc=tc, bank=bank, ci=ci:
                       E.activation(dstT[:, tc:tc + 1],
                                    bank[0:128, ci:ci + 1], ACT.Copy),
                       "s" if k == len(dsts) - 1 else None)
            bankv[b] = ("s", sch.cnt["s"])
        sch.wait("vector", "s", sch.cnt["s"])
        return sch.op("vector", lambda E: E.engine_nop(), "v")

    v_r = pair_to_cols(CV, cqmT, ckvT, v_cv, 0)
    v_cqkeep = sch.op("vector", lambda E: E.tensor_copy(
        maskr[0:1, 0:NT], CV[0:1, :]), "v")
    v_ckvT = pair_to_cols(ERR, scqT, None, max(v_r, v_cqkeep), 0)

    with ExitStack() as tstack:
        iotab = sb(tstack, nc, tag + "_iob", [128, NT], F32)
        bsc = sb(tstack, nc, tag + "_bsc", [128, NT], F32)
        cnt = sb(tstack, nc, tag + "_cnt", [128, 48], F32)
        sch.fence("sync")
        sch.dma("sync", iotab, dram_w["IOTAB"], "ds")
        ds_io = sch.cnt["ds"]
        sch.wait("tensor", "v", v_ckvT)
        for cb in range(4):
            b = 4 + (cb % 2)
            bank = twrite(b)
            sch.op("tensor", lambda E, bank=bank, cb=cb: E.matmul(
                bank, P.ones_row, ERR[0:1, cb * 512:(cb + 1) * 512],
                start=True, stop=True), "t")
            t_t = sch.cnt["t"]
            sch.wait("vector", "t", t_t)
            bankv[b] = sch.op("vector", lambda E, bank=bank, cb=cb:
                              E.tensor_copy(bsc[:, cb * 512:(cb + 1) * 512],
                                            bank), "v")
        v_bsc = sch.cnt["v"]
        sch.wait("vector", "v", max(v_bsc, v_ckvT))
        sch.wait("vector", "ds", ds_io)
        for jc in range(16):
            sch.op("vector", lambda E, jc=jc: E.tensor_scalar(
                tmp[0], bsc, scqT[:, jc:jc + 1], None, ALU.is_gt))
            sch.op("vector", lambda E, jc=jc: E.reduce_sum(
                cnt[:, jc:jc + 1], tmp[0], AX.X))
            sch.op("vector", lambda E, jc=jc: E.tensor_scalar(
                tmp[0], bsc, scqT[:, jc:jc + 1], None, ALU.is_equal))
            sch.op("vector", lambda E, jc=jc: E.tensor_scalar(
                tmp[1], iotab, jT[:, jc:jc + 1], None, ALU.is_lt))
            sch.op("vector", lambda E: E.tensor_tensor(
                tmp[0], tmp[0], tmp[1], ALU.mult))
            sch.op("vector", lambda E, jc=jc: E.reduce_sum(
                cnt[:, 16 + jc:17 + jc], tmp[0], AX.X), "v")
        sch.selfwait("vector", "v")
        sch.op("vector", lambda E: E.tensor_tensor(
            cnt[:, 32:48], cnt[:, 0:16], cnt[:, 16:32], ALU.add), "v")
        sch.selfwait("vector", "v")
        v_mask = sch.op("vector", lambda E: E.tensor_scalar(
            cnt[:, 32:48], cnt[:, 32:48], 1024.0, None, ALU.is_lt), "v")
        sch.selfwait("vector", "v")
        v_cqmT = sch.op("vector", lambda E: E.tensor_tensor(
            cqmT, cqmT, cnt[:, 32:48], ALU.mult), "v")
        sch.wait("tensor", "v", v_cqmT)
        for tc in range(16):
            b = 2 + (tc % 2)
            bank = twrite(b)
            sch.op("tensor", lambda E, bank=bank, tc=tc: E.transpose(
                bank[0:1, 0:128], cnt[:, 32 + tc:33 + tc], P.ident), "t")
            t_t = sch.cnt["t"]
            sch.wait("scalar", "t", t_t)
            bankv[b] = ("s", sch.op("scalar", lambda E, bank=bank, tc=tc:
                                    E.activation(
                tmp[0][0:1, tc * 128:(tc + 1) * 128], bank[0:1, 0:128],
                ACT.Copy), "s"))
        sch.wait("vector", "s", sch.cnt["s"])
        v_cqm = sch.op("vector", lambda E: E.tensor_tensor(
            cqrow, tmp[0][0:1, 0:NT], maskr[0:1, 0:NT], ALU.mult), "v")

    with ExitStack() as hstack:
        q_hi = sb(hstack, nc, tag + "_qhh", [128, 4 * NT], BF16)
        q_lo = sb(hstack, nc, tag + "_qhl", [128, 4 * NT], BF16)
        k_hi = sb(hstack, nc, tag + "_khh", [128, 4 * NT], BF16)
        k_lo = sb(hstack, nc, tag + "_khl", [128, 4 * NT], BF16)

        xhstate = {"v": v_xblk_free, "s": s_xblk_free}

        def xh_block(pstack_xh_hi, pstack_xh_lo, col0, v_wready):
            sch.wait("sync", "v", xhstate["v"])
            sch.wait("sync", "s", xhstate["s"])
            for c in range(NCH):
                sch.dma("sync", xblk[:, c * 512:(c + 1) * 512],
                        x_src[c * 128:(c + 1) * 128, col0:col0 + 512], "ds")
            ds_x = sch.cnt["ds"]
            sch.wait("tensor", "ds", ds_x)
            sch.wait("tensor", "v",
                     max(bankv.get(4, 0), bankv.get(5, 0), v_wready))
            b4 = twrite(4)
            sch.op("tensor", lambda E, b4=b4, col0=col0: E.matmul(
                b4, P.ones_row, wides[0:1, col0:col0 + 512],
                start=True, stop=True))
            b5 = twrite(5)
            t_bc = sch.op("tensor", lambda E, b5=b5, col0=col0: E.matmul(
                b5, P.ones_row, wides[0:1, NT + col0:NT + col0 + 512],
                start=True, stop=True), "t")
            sch.wait("vector", "t", t_bc)
            for c in range(NCH):
                xs = xblk[:, c * 512:(c + 1) * 512]
                t0 = tmp[c % 2][:, 0:512]
                hi = pstack_xh_hi[:, c * 512:(c + 1) * 512]
                lo = pstack_xh_lo[:, c * 512:(c + 1) * 512]
                sch.op("vector", lambda E, xs=xs, t0=t0: E.tensor_tensor(
                    t0, xs, pb[4], ALU.subtract))
                sch.op("vector", lambda E, t0=t0: E.tensor_tensor(
                    t0, t0, pb[5], ALU.mult))
                sch.op("vector", lambda E, t0=t0, c=c: E.tensor_scalar_mul(
                    t0, t0, gh[:, c:c + 1]))
                sch.op("vector", lambda E, hi=hi, t0=t0: E.tensor_copy(hi, t0))
                sch.op("vector", lambda E, hi=hi, lo=lo, t0=t0:
                       E.tensor_tensor(lo, t0, hi, ALU.subtract), "v")
            v_xh = sch.cnt["v"]
            bankv[4] = bankv[5] = v_xh
            xhstate["v"] = v_xh
            xhstate["s"] = sch.cnt["s"]
            return v_xh

        def load_split(pfx, wth, wtl, nmsrc, wid, wstg, c0, c1, v_w0=0):
            v_w = v_w0
            for ic in range(NCH):
                if v_w:
                    sch.wait("sync", "v", v_w)
                dsw = sch.dma("sync", wstg[:, 0:wid],
                              dram_w[nmsrc][ic * 128:(ic + 1) * 128, c0:c1],
                              "ds")
                sch.wait("vector", "ds", dsw)
                sch.op("vector", lambda E, ic=ic, wth=wth, wid=wid,
                       wstg=wstg: E.tensor_copy(
                    wth[:, ic * wid:(ic + 1) * wid], wstg[:, 0:wid]), "v")
                v_w = sch.op("vector", lambda E, ic=ic, wth=wth, wtl=wtl,
                             wid=wid, wstg=wstg: E.tensor_tensor(
                    wtl[:, ic * wid:(ic + 1) * wid], wstg[:, 0:wid],
                    wth[:, ic * wid:(ic + 1) * wid], ALU.subtract), "v")
            return v_w

        with ExitStack() as pstack:
            wstg = sb(pstack, nc, tag + "_wsg", [128, 512], F32)
            wq_hi = sb(pstack, nc, tag + "_hqh", [128, NCH * 512], BF16)
            wq_lo = sb(pstack, nc, tag + "_hql", [128, NCH * 512], BF16)
            wk_hi = sb(pstack, nc, tag + "_hkh", [128, NCH * 512], BF16)
            wk_lo = sb(pstack, nc, tag + "_hkl", [128, NCH * 512], BF16)
            xh_hi = sb(pstack, nc, tag + "_xhh", [128, NCH * 512], BF16)
            xh_lo = sb(pstack, nc, tag + "_xhl", [128, NCH * 512], BF16)
            sch.fence("sync")
            v_w = load_split("q", wq_hi, wq_lo, "WQH", 512, wstg, 0, 512)
            v_w = load_split("k", wk_hi, wk_lo, "WKVHP", 512,
                             wstg, 0, 512, v_w0=v_w)
            for tb in range(4):
                col0 = tb * 512
                v_xh = xh_block(xh_hi, xh_lo, col0, v_w if tb == 0 else 0)
                sch.wait("tensor", "v", v_xh)
                for (dh, dl, wh, wl) in ((q_hi, q_lo, wq_hi, wq_lo),
                                         (k_hi, k_lo, wk_hi, wk_lo)):
                    for oc in range(4):
                        b = oc % 2
                        bank = twrite(b)
                        n_mm = 0
                        for wp, xp in ((wh, xh_hi), (wh, xh_lo), (wl, xh_hi)):
                            for ic in range(NCH):
                                n_mm += 1
                                sch.op("tensor", lambda E, bank=bank, wp=wp,
                                       xp=xp, ic=ic, oc=oc, st=(n_mm == 1),
                                       sp=(n_mm == 24): E.matmul(
                                    bank,
                                    wp[:, ic * 512 + oc * 128:ic * 512 + (oc + 1) * 128],
                                    xp[:, ic * 512:(ic + 1) * 512],
                                    start=st, stop=sp),
                                    "t" if n_mm == 24 else None)
                        t_g = sch.cnt["t"]
                        sch.wait("vector", "t", t_g)
                        hi = dh[:, oc * NT + col0:oc * NT + col0 + 512]
                        lo = dl[:, oc * NT + col0:oc * NT + col0 + 512]
                        sch.op("vector", lambda E, hi=hi, bank=bank:
                               E.tensor_copy(hi, bank))
                        bankv[b] = sch.op("vector", lambda E, hi=hi, lo=lo,
                                          bank=bank: E.tensor_tensor(
                            lo, bank, hi, ALU.subtract), "v")
        v_hi_t = sb(hstack, nc, tag + "_vph", [128, 16 * 520], BF16)
        v_lo_t = sb(hstack, nc, tag + "_vpl", [128, 16 * 520], BF16)
        with ExitStack() as pstack:
            wstg = sb(pstack, nc, tag + "_wsg2", [128, 512], F32)
            wv_hi = sb(pstack, nc, tag + "_hvh", [128, NCH * 512], BF16)
            wv_lo = sb(pstack, nc, tag + "_hvl", [128, NCH * 512], BF16)
            xh_hi = sb(pstack, nc, tag + "_xh2", [128, NCH * 512], BF16)
            xh_lo = sb(pstack, nc, tag + "_xl2", [128, NCH * 512], BF16)
            sch.fence("sync")
            v_w = load_split("v", wv_hi, wv_lo, "WKVHP", 512, wstg, 512, 1024)
            sch.wait("vector", "v", v_ckvT)
            for tb in range(4):
                col0 = tb * 512
                v_xh = xh_block(xh_hi, xh_lo, col0, v_w if tb == 0 else 0)
                sch.wait("tensor", "v", v_xh)
                for ts_ in range(4):
                    tglob = tb * 4 + ts_
                    b = 2 + (ts_ % 2)
                    bank = twrite(b)
                    n_mm = 0
                    for xp, wp in ((xh_hi, wv_hi), (xh_lo, wv_hi),
                                   (xh_hi, wv_lo)):
                        for ic in range(NCH):
                            n_mm += 1
                            sch.op("tensor", lambda E, bank=bank, xp=xp, wp=wp,
                                   ic=ic, ts_=ts_, st=(n_mm == 1),
                                   sp=(n_mm == 24): E.matmul(
                                bank,
                                xp[:, ic * 512 + ts_ * 128:ic * 512 + (ts_ + 1) * 128],
                                wp[:, ic * 512:(ic + 1) * 512],
                                start=st, stop=sp),
                                "t" if n_mm == 24 else None)
                    t_g = sch.cnt["t"]
                    sch.wait("vector", "t", t_g)
                    for h in range(8):
                        hi = v_hi_t[:, tglob * 520 + h * 65:tglob * 520 + h * 65 + 64]
                        lo = v_lo_t[:, tglob * 520 + h * 65:tglob * 520 + h * 65 + 64]
                        srcv = bank[:, h * 64:(h + 1) * 64]
                        sch.op("vector", lambda E, hi=hi, srcv=srcv:
                               E.tensor_copy(hi, srcv))
                        sch.op("vector", lambda E, hi=hi, lo=lo, srcv=srcv:
                               E.tensor_tensor(lo, srcv, hi, ALU.subtract),
                               "v")
                    bankv[b] = sch.cnt["v"]
                    sch.op("vector", lambda E, tglob=tglob: E.memset(
                        bass.AP(v_hi_t.tensor, v_hi_t.offset + tglob * 520 + 64,
                                [[16 * 520, 128], [65, 8]]), 1.0))
                    sch.op("vector", lambda E, tglob=tglob: E.memset(
                        bass.AP(v_lo_t.tensor, v_lo_t.offset + tglob * 520 + 64,
                                [[16 * 520, 128], [65, 8]]), 0.0), "v")
                    if DBG_P2BARRIER:
                        sch.barrier()
        v_vaug = sch.cnt["v"]
        if "DBG_VP0" in dram_w:
            sch.wait("sync", "v", v_vaug)
            sch.dma("sync", dram_w["DBG_VP0"], v_hi_t, "ds")
            sch.dma("sync", dram_w["DBG_VP0L"], v_lo_t, "ds")
        v_negM = emit_negM(q_hi, q_lo, k_hi, k_lo, NT, 4, -0.125)
        sch.selfwait("vector", "v")
        if "DBG_VP1" in dram_w:
            sch.wait("sync", "v", sch.cnt["v"])
            sch.dma("sync", dram_w["DBG_VP1"], v_hi_t, "ds")
            sch.dma("sync", dram_w["DBG_VP1L"], v_lo_t, "ds")

        E_hi = sb(hstack, nc, tag + "_Eh", [128, 16 * 256], BF16)
        E_lo = sb(hstack, nc, tag + "_El", [128, 16 * 256], BF16)
        aoT_hi = sb(hstack, nc, tag + "_aoh", [128, 4 * 256], BF16)
        aoT_lo = sb(hstack, nc, tag + "_aol", [128, 4 * 256], BF16)
        wo_hi = sb(hstack, nc, tag + "_woh", [128, 4 * 1024], BF16)
        wo_lo = sb(hstack, nc, tag + "_wol", [128, 4 * 1024], BF16)
        facr = sb(hstack, nc, tag + "_fac", [1, 512], F32)
        facdbg = rout[0:1, :] if "DBG_FACALL" in dram_w else None
        avsdbg = xblk[0:65, 2048:4096] if "DBG_FACALL" in dram_w else None
        lblk = sb(hstack, nc, tag + "_lb", [128, 256], F32)
        wstg2 = sb(hstack, nc, tag + "_ws2", [128, 512], F32)
        sthi2 = sb(hstack, nc, tag + "_sh2", [128, 256], BF16)
        stlo2 = sb(hstack, nc, tag + "_sl2", [128, 256], BF16)
        v_w2 = 0
        sch.fence("sync")
        for hc in range(4):
            for hf in range(2):
                if v_w2:
                    sch.wait("sync", "v", v_w2)
                dsw = sch.dma("sync", wstg2,
                              dram_w["WOH"][hc * 128:(hc + 1) * 128,
                                            hf * 512:(hf + 1) * 512], "ds")
                sch.wait("vector", "ds", dsw)
                c0 = hc * 1024 + hf * 512
                sch.op("vector", lambda E, c0=c0: E.tensor_copy(
                    wo_hi[:, c0:c0 + 512], wstg2), "v")
                v_w2 = sch.op("vector", lambda E, c0=c0: E.tensor_tensor(
                    wo_lo[:, c0:c0 + 512], wstg2,
                    wo_hi[:, c0:c0 + 512], ALU.subtract), "v")

        sch.wait("tensor", "v", max(v_vaug, v_cqm, v_w2, v_negM))
        sch.wait("scalar", "v", max(v_negM, v_cqm))
        sch.wait("sync", "da", sch.cnt["da"])
        da_hist = []
        vsplit_hist = [0, 0]
        v_lb_free = v_vaug
        IBW = 256
        for ib in range(NT // IBW):
            icol = ib * IBW
            for h in range(8):
                hc, po = h // 2, (h % 2) * 64
                for jc in range(16):
                    b = jc % 2
                    bank = twrite(b)
                    n_mm = 0
                    for kp, qp in ((k_hi, q_hi), (k_hi, q_lo), (k_lo, q_hi)):
                        n_mm += 1
                        sch.op("tensor", lambda E, bank=bank, kp=kp, qp=qp,
                               jc=jc, hc=hc, po=po, icol=icol,
                               st=(n_mm == 1), sp=(n_mm == 3): E.matmul(
                            bank[:, 0:IBW],
                            kp[po:po + 64,
                               hc * NT + jc * 128:hc * NT + (jc + 1) * 128],
                            qp[po:po + 64, hc * NT + icol:hc * NT + icol + IBW],
                            start=st, stop=sp), "t" if n_mm == 3 else None)
                    t_si = sch.cnt["t"]
                    sch.wait("scalar", "t", t_si)
                    eb = tmp[0][:, (jc % 2) * IBW:(jc % 2 + 1) * IBW]
                    sch.wait("scalar", "v", vsplit_hist[jc % 2])
                    s_ex = sch.op("scalar", lambda E, bank=bank, h=h, eb=eb:
                                  E.activation(
                        eb, bank[:, 0:IBW], ACT.Exp,
                        bias=negMh[:, h:h + 1], scale=0.125), "s")
                    bankv[b] = ("s", s_ex)
                    sch.wait("vector", "s", s_ex)
                    ehi = E_hi[:, jc * IBW:(jc + 1) * IBW]
                    elo = E_lo[:, jc * IBW:(jc + 1) * IBW]
                    sch.op("vector", lambda E, ehi=ehi, eb=eb: E.tensor_copy(
                        ehi, eb))
                    vsplit_hist[jc % 2] = sch.op(
                        "vector", lambda E, ehi=ehi, elo=elo, eb=eb:
                        E.tensor_tensor(elo, eb, ehi, ALU.subtract), "v")
                v_E = sch.cnt["v"]
                if "DBG_EH" in dram_w and ib == 0 and h == 7:
                    sch.wait("gpsimd", "v", v_E)
                    sch.dma("gpsimd", dram_w["DBG_EH"], E_hi, "dg")
                    sch.dma("gpsimd", dram_w["DBG_EL"], E_lo, "dg")
                sch.wait("tensor", "v", v_E)
                b2 = 2 + (h % 2)
                bank2 = twrite(b2)
                n_mm = 0
                for vp, ep in ((v_hi_t, E_hi), (v_hi_t, E_lo), (v_lo_t, E_hi)):
                    for jc in range(16):
                        n_mm += 1
                        sch.op("tensor", lambda E, bank2=bank2, vp=vp, ep=ep,
                               jc=jc, h=h, st=(n_mm == 1), sp=(n_mm == 48):
                               E.matmul(
                            bank2[0:65, 0:IBW],
                            vp[:, jc * 520 + h * 65:jc * 520 + (h + 1) * 65],
                            ep[:, jc * IBW:(jc + 1) * IBW],
                            start=st, stop=sp))
                t_av = tail_pad()
                sch.wait("vector", "t", t_av)
                avs = tmp[1][0:65, 512:512 + IBW]
                bankv[b2] = sch.op("vector", lambda E, bank2=bank2, avs=avs:
                                   E.tensor_copy(avs, bank2[0:65, 0:IBW]), "v")
                sch.selfwait("vector", "v")
                sch.op("vector", lambda E, avs=avs: E.reciprocal(
                    facr[0:1, 0:IBW], avs[64:65, :]), "v")
                sch.selfwait("vector", "v")
                sch.op("vector", lambda E, icol=icol: E.tensor_copy(
                    tmp[1][0:1, 1536:1536 + IBW],
                    cqrow[0:1, icol:icol + IBW]))
                sch.selfwait("vector", "v")
                v_f = sch.op("vector", lambda E: E.tensor_tensor(
                    facr[0:1, 0:IBW], facr[0:1, 0:IBW],
                    tmp[1][0:1, 1536:1536 + IBW], ALU.mult), "v")
                if "DBG_EH" in dram_w and ib == 0 and h == 7:
                    sch.wait("gpsimd", "v", v_f)
                    sch.dma("gpsimd", dram_w["DBG_FAC"], facr, "dg")
                if facdbg is not None and ib == 0:
                    sch.op("vector", lambda E, h=h: E.tensor_copy(
                        facdbg[0:1, h * IBW:(h + 1) * IBW],
                        facr[0:1, 0:IBW]))
                    sch.op("vector", lambda E, h=h, avs=avs: E.tensor_copy(
                        avsdbg[:, h * IBW:(h + 1) * IBW], avs), "v")
                sch.wait("tensor", "v", v_f)
                b6 = twrite(6)
                sch.op("tensor", lambda E, b6=b6: E.matmul(
                    b6[0:64, 0:IBW], P.ones_row[0:1, 0:64],
                    facr[0:1, 0:IBW], start=True, stop=True), "t")
                t_fb = sch.cnt["t"]
                sch.wait("vector", "t", t_fb)
                sch.op("vector", lambda E: E.tensor_copy(
                    tmp[1][0:64, 1024:1024 + IBW], pb[6][0:64, 0:IBW]), "v")
                sch.selfwait("vector", "v")
                sch.op("vector", lambda E, avs=avs: E.tensor_tensor(
                    tmp[0][0:64, 512:512 + IBW], avs[0:64, :],
                    tmp[1][0:64, 1024:1024 + IBW], ALU.mult), "v")
                sch.selfwait("vector", "v")
                dsth = aoT_hi[po:po + 64, hc * IBW:(hc + 1) * IBW]
                dstl = aoT_lo[po:po + 64, hc * IBW:(hc + 1) * IBW]
                sch.op("vector", lambda E: E.tensor_copy(
                    sthi2[0:64, 0:IBW], tmp[0][0:64, 512:512 + IBW]))
                sch.op("vector", lambda E: E.tensor_tensor(
                    stlo2[0:64, 0:IBW], tmp[0][0:64, 512:512 + IBW],
                    sthi2[0:64, 0:IBW], ALU.subtract), "v")
                sch.selfwait("vector", "v")
                sch.op("vector", lambda E, dsth=dsth: E.tensor_copy(
                    dsth, sthi2[0:64, 0:IBW]))
                bankv[6] = sch.op(
                    "vector", lambda E, dstl=dstl: E.tensor_copy(
                        dstl, stlo2[0:64, 0:IBW]), "v")
            v_ao = sch.cnt["v"]
            if "DBG_EH" in dram_w and ib == 0:
                sch.wait("gpsimd", "v", v_ao)
                sch.dma("gpsimd", dram_w["DBG_AOT"], aoT_hi, "dg")
                sch.dma("gpsimd", dram_w["DBG_AOL"], aoT_lo, "dg")
                sch.dma("gpsimd", dram_w["DBG_QH"], q_hi, "dg")
                sch.dma("gpsimd", dram_w["DBG_KH"], k_hi, "dg")
            if facdbg is not None and ib == 0:
                sch.wait("gpsimd", "v", v_ao)
                sch.dma("gpsimd", dram_w["DBG_FACALL"], facdbg, "dg")
                sch.dma("gpsimd", dram_w["DBG_AVSALL"][0:65, :], avsdbg, "dg")
                if "DBG_VHT" in dram_w:
                    sch.dma("gpsimd", dram_w["DBG_VHT"], v_hi_t, "dg")
                    sch.dma("gpsimd", dram_w["DBG_VLT"], v_lo_t, "dg")
            if facdbg is not None and ib == 4 and "DBG_VHT2" in dram_w:
                sch.wait("gpsimd", "v", v_ao)
                sch.dma("gpsimd", dram_w["DBG_VHT2"], v_hi_t, "dg")
                sch.dma("gpsimd", dram_w["DBG_VLT2"], v_lo_t, "dg")
            if "DBG_VHTS" in dram_w and ib in (0, 4):
                key = "DBG_VHTS" if ib == 0 else "DBG_VHTS2"
                sch.wait("sync", "v", v_ao)
                sch.dma("sync", dram_w[key], v_hi_t, "ds")
                sch.dma("sync", dram_w[key.replace("VHT", "VLT")],
                        v_lo_t, "ds")
            sch.wait("tensor", "v", v_ao)
            for dc in range(NCH):
                sch.wait("sync", "v", v_lb_free)
                dsl = sch.dma("sync", lblk[:, 0:IBW],
                              lacc_hbm[dc * 128:(dc + 1) * 128,
                                       icol:icol + IBW], "ds")
                dsx = sch.dma("sync", xblk[:, 0:IBW],
                              x_src[dc * 128:(dc + 1) * 128, icol:icol + IBW],
                              "ds")
                b = 4 + (dc % 2)
                bank = twrite(b)
                n_mm = 0
                for wp, ap_ in ((wo_hi, aoT_hi), (wo_hi, aoT_lo),
                                (wo_lo, aoT_hi)):
                    for hc in range(4):
                        n_mm += 1
                        sch.op("tensor", lambda E, bank=bank, wp=wp, ap_=ap_,
                               hc=hc, dc=dc, st=(n_mm == 1), sp=(n_mm == 12):
                               E.matmul(
                            bank[:, 0:IBW],
                            wp[:, hc * 1024 + dc * 128:hc * 1024 + (dc + 1) * 128],
                            ap_[:, hc * IBW:(hc + 1) * IBW],
                            start=st, stop=sp), "t" if n_mm == 12 else None)
                t_wo = sch.cnt["t"]
                sch.wait("vector", "t", t_wo)
                sch.wait("vector", "ds", max(dsl, dsx))
                if len(da_hist) >= 2:
                    sch.wait("vector", "da", da_hist[-2])
                sch.op("vector", lambda E, bank=bank: E.tensor_add(
                    lblk[:, 0:IBW], bank[:, 0:IBW], lblk[:, 0:IBW]), "v")
                v_lb_free = sch.cnt["v"] + 1
                xo = tmp[1][:, (dc % 2) * IBW:(dc % 2) * IBW + IBW]
                v_c = sch.op("vector", lambda E, xo=xo: E.scalar_tensor_tensor(
                    xo, lblk[:, 0:IBW], 0.5, xblk[:, 0:IBW],
                    ALU.mult, ALU.add), "v")
                bankv[b] = v_c
                sch.wait("scalar", "v", v_c)
                da_hist.append(sch.dma(
                    "scalar", x_dst[dc * 128:(dc + 1) * 128, icol:icol + IBW],
                    xo, "da"))
    sch.wait("scalar", "da", sch.cnt["da"])
    sch.dbg = dict(rout=rout, cqmT=cqmT, ckvT=ckvT, scqT=scqT, negMh=negMh,
                   stat=stat, mh=mh, wides=wides)
    sch.dbg_l = getattr(sch, 'dbg_l', {})


def prep_l1ffn_inputs(gamma_light, wq_light, wkv_light, wo_light,
                      q_rt, kv_rt, gamma_heavy, wq_heavy, wkv_heavy, wo_heavy):
    perm = np.zeros(1024, dtype=np.int64)
    for h in range(8):
        for dd in range(64):
            perm[h * 64 + dd] = h * 128 + dd
            perm[512 + h * 64 + dd] = h * 128 + 64 + dd
    WKVHP = np.ascontiguousarray(wkv_heavy[perm, :].T)
    sc = lambda s: np.ascontiguousarray(s.reshape(-1, 128).T)
    JT = np.ascontiguousarray(
        (np.arange(16)[None, :] * 128 + np.arange(128)[:, None])
        .astype(np.float32))
    IOTAB = np.ascontiguousarray(
        np.tile(np.arange(2048, dtype=np.float32), (128, 1)))
    HSEL = np.zeros((128, 32), np.float32)
    for oc in range(4):
        HSEL[0:64, oc * 8 + 2 * oc] = 1.0
        HSEL[64:128, oc * 8 + 2 * oc + 1] = 1.0
    LOGK = np.array([[np.float32(np.log(np.float64(1152.0)))],
                     [np.float32(np.log(np.float64(2048.0)))]], np.float32)
    return dict(
        RT=np.ascontiguousarray(np.stack([q_rt, kv_rt], 1)),
        GL=sc(gamma_light), GH=sc(gamma_heavy),
        WQL=np.ascontiguousarray(wq_light),
        WKVL=np.ascontiguousarray(wkv_light),
        WOL=np.ascontiguousarray(wo_light),
        WQH=np.ascontiguousarray(wq_heavy), WKVHP=WKVHP,
        WOH=np.ascontiguousarray(wo_heavy),
        LOGK=LOGK, JT=JT, IOTAB=IOTAB, HSEL=HSEL,
    )


def prep_ffn0_weights(w1, s1, w2, s2, h1, sh1, h2, sh2, w3, s3):
    def panel2(w):
        return np.ascontiguousarray(
            w.T.reshape(8, 128, 32, 128).transpose(2, 1, 0, 3)
            .reshape(32, 128, 1024))

    def panelH(w):
        return np.ascontiguousarray(
            w.T.reshape(32, 128, 32, 128).transpose(2, 1, 0, 3)
            .reshape(32, 128, 4096))

    def panel3(w):
        return np.ascontiguousarray(
            w.T.reshape(32, 128, 8, 128).transpose(2, 1, 0, 3)
            .reshape(8, 128, 4096))

    def sc(s):
        return np.ascontiguousarray(s.reshape(-1, 128).T)

    return dict(
        W1P=panel2(w1), W2P=panel2(w2), H1P=panelH(h1), H2P=panelH(h2),
        W3P=panel3(w3), ffn_s1=sc(s1), ffn_s2=sc(s2), ffn_sh1=sc(sh1),
        ffn_sh2=sc(sh2), ffn_s3=sc(s3),
    )


def prep_const_inputs():
    consts = np.tile(np.array(CONST_VALS, dtype=np.float32), (128, 1))
    ident = np.eye(128, dtype=np.float32)
    return dict(consts=consts, ident=ident)


class PState(Prelude):
    def __init__(self, nc, sch):
        super().__init__(nc, sch)
        self.pb = [nc.alloc_psum_tensor(f"pb{i}", [128, 512], F32).ap()
                   for i in range(8)]


def _build_megakernel():
    nc = bass.Bass("TRN2", target_bir_lowering=False)
    sch = Sch(nc)
    dram = {}
    dram["consts"] = nc.dram_tensor(
        "consts", [128, len(CONST_VALS)], F32, kind="ExternalInput").ap()
    dram["ident"] = nc.dram_tensor(
        "ident", [128, 128], F32, kind="ExternalInput").ap()
    for nm, shape in [("ffn_s1", [128, 32]), ("ffn_s2", [128, 32]),
                      ("ffn_sh1", [128, 32]), ("ffn_sh2", [128, 32]),
                      ("ffn_s3", [128, 8])]:
        dram[nm] = nc.dram_tensor(nm, shape, F32, kind="ExternalInput").ap()
    wd_ffn = {}
    for nm, shape in [("W1P", [32, 128, 1024]), ("W2P", [32, 128, 1024]),
                      ("H1P", [32, 128, 4096]), ("H2P", [32, 128, 4096]),
                      ("W3P", [8, 128, 4096])]:
        wd_ffn[nm] = nc.dram_tensor(nm, shape, F32, kind="ExternalInput").ap()
    wd_conv = {}
    for nm, shape in [("KTP", [8, 128, 2048]), ("FWD", [128, 4096]),
                      ("FNY", [128, 4]), ("FINV", [128, 8192]),
                      ("FINVNY", [1, 1024]), ("SKIP", [1, 1024])]:
        wd_conv[nm] = nc.dram_tensor(nm, shape, F32, kind="ExternalInput").ap()
    wd_attn = {}
    for nm, shape in [("WQK", [1024, 128]), ("WVT", [1024, 1024]),
                      ("GB", [128, 4]), ("BQK", [128, 1]), ("BVB", [128, 1024]),
                      ("REL", [32, 1]), ("GREV", [32, 512])]:
        wd_attn[nm] = nc.dram_tensor(nm, shape, F32, kind="ExternalInput").ap()
    wd_lffn = {}
    for nm, shape in [("RT", [1024, 2]), ("GL", [128, 8]), ("GH", [128, 8]),
                      ("WQL", [1024, 512]), ("WKVL", [1024, 1024]),
                      ("WOL", [512, 1024]), ("WQH", [1024, 512]),
                      ("WKVHP", [1024, 1024]), ("WOH", [512, 1024]),
                      ("LOGK", [2, 1]), ("JT", [128, 16]),
                      ("IOTAB", [128, 2048]), ("HSEL", [128, 32])]:
        wd_lffn[nm] = nc.dram_tensor(nm, shape, F32, kind="ExternalInput").ap()
    x_in = nc.dram_tensor("x_in", [1024, 2048], F32, kind="ExternalInput").ap()
    x_out = nc.dram_tensor("x_out", [1024, 2048], F32,
                           kind="ExternalOutput").ap()
    xa = nc.dram_tensor("xa", [1024, 2048], F32, kind="Internal").ap()
    xb = nc.dram_tensor("xb", [1024, 2048], F32, kind="Internal").ap()
    lacc = nc.dram_tensor("lacc", [1024, 2048], F32, kind="Internal").ap()
    bias_hbm = nc.dram_tensor("bias_hbm", [1, 1024], F32, kind="Internal").ap()

    with nc.Block() as block:
        P = PState(nc, sch)
        P.emit_load(sch, dram)
        chain = [
            (emit_ffn0, (wd_ffn, x_in, xa, "f0")),
            (emit_conv, (wd_conv, xa, xb, "cv")),
            (emit_ffn0, (wd_ffn, xb, xa, "f1")),
            (emit_l1ffn, (wd_lffn, xa, xb, lacc, "g0")),
            (emit_l1attn, (wd_attn, xb, xa, bias_hbm, "a1")),
            (emit_l1ffn, (wd_lffn, xa, x_out, lacc, "g1")),
        ]
        for fn, args in chain:
            with ExitStack() as sstack:
                fn(sch, P, sstack, *args)
            sch.barrier()
        sch.finalize(block)
    return nc


def kernel(**inputs):
    x = np.asarray(inputs["L0_attn_u"], dtype=np.float32)
    common = {}
    common.update(prep_const_inputs())
    common.update(prep_ffn0_weights(
        inputs["L0_ffn_w1_w"], inputs["L0_ffn_w1_s"], inputs["L0_ffn_w2_w"],
        inputs["L0_ffn_w2_s"], inputs["L0_ffn_h1_w"], inputs["L0_ffn_h1_s"],
        inputs["L0_ffn_h2_w"], inputs["L0_ffn_h2_s"], inputs["L0_ffn_w3_w"],
        inputs["L0_ffn_w3_s"]))
    common.update(prep_conv_inputs(inputs["L0_attn_k"],
                                   inputs["L0_attn_skip"]))
    common.update(prep_l1attn_inputs(
        inputs["L1_attn_Wqk"], inputs["L1_attn_bqk"], inputs["L1_attn_Wv"],
        inputs["L1_attn_bv"], inputs["L1_attn_gamma"], inputs["L1_attn_beta"],
        inputs["L1_attn_rel_emb"]))
    common.update(prep_l1ffn_inputs(
        inputs["L1_ffn_gamma_light"], inputs["L1_ffn_wq_light"],
        inputs["L1_ffn_wkv_light"], inputs["L1_ffn_wo_light"],
        inputs["L1_ffn_q_routing_token"], inputs["L1_ffn_kv_routing_token"],
        inputs["L1_ffn_gamma_heavy"], inputs["L1_ffn_wq_heavy"],
        inputs["L1_ffn_wkv_heavy"], inputs["L1_ffn_wo_heavy"]))
    in_maps = []
    for b in range(8):
        im = dict(common)
        im["x_in"] = np.ascontiguousarray(x[b].T)
        in_maps.append(im)
    nc = _build_megakernel()
    res = run_bass_kernel_spmd(nc, in_maps, core_ids=list(range(8)))
    out = np.stack([np.asarray(res.results[b]["x_out"]).T for b in range(8)],
                   axis=0)
    return np.ascontiguousarray(out).astype(np.float32)
